# version 29
# baseline (speedup 1.0000x reference)
import os
import numpy as np
import ml_dtypes
import contextlib
import concourse.bass as bass
import concourse.mybir as mybir
from concourse.bass_utils import run_bass_kernel_spmd

F32 = mybir.dt.float32
BF16 = mybir.dt.bfloat16
AF = mybir.ActivationFunctionType
ALU = mybir.AluOpType

T = int(os.environ.get("K_T", "4096"))
D = 1024
HC = 8
DI = 512
NP = 4
NH = 8
DH = 64
E = 8
I_FF = 2048
IC = 16
TT = 512
NTT = T // TT
BFD = T // 128
C = 128
NCAP = max(512, ((3 * T // 8 + 511) // 512) * 512)
NB = NCAP // 128
NCH = TT // C
LN_EPS = 1e-5

PHASE2 = os.environ.get("K_PHASE2", "sparse")


def build_nc():
    nc = bass.Bass("TRN2", target_bir_lowering=False, debug=False)

    x_fm = nc.dram_tensor("x_fm", [D, T], F32, kind="ExternalInput").ap()
    wq = nc.dram_tensor("wq", [128, HC, DI], F32, kind="ExternalInput").ap()
    wk = nc.dram_tensor("wk", [128, HC, DI], F32, kind="ExternalInput").ap()
    wv = nc.dram_tensor("wv", [128, HC, DI], F32, kind="ExternalInput").ap()
    wa = nc.dram_tensor("wa", [128, HC, DI], F32, kind="ExternalInput").ap()
    ba = nc.dram_tensor("ba", [128, NP], F32, kind="ExternalInput").ap()
    wo = nc.dram_tensor("wo", [128, NP, D], F32, kind="ExternalInput").ap()
    gmb = nc.dram_tensor("gmb", [128, 2, HC], F32, kind="ExternalInput").ap()
    wgt = nc.dram_tensor("wgt", [128, HC, E], F32, kind="ExternalInput").ap()
    w1t = nc.dram_tensor("w1t", [E, 128, HC, I_FF], F32, kind="ExternalInput").ap()
    w3t = nc.dram_tensor("w3t", [E, 128, HC, I_FF], F32, kind="ExternalInput").ap()
    w2t = nc.dram_tensor("w2t", [E, 128, IC, D], F32, kind="ExternalInput").ap()
    cst = nc.dram_tensor("cst", [128, 1544], F32, kind="ExternalInput").ap()
    tokc = nc.dram_tensor("tokc", [128, T // 128], F32, kind="ExternalInput").ap()
    out = nc.dram_tensor("out", [T + 1, D], F32, kind="ExternalOutput").ap()
    x2ntm = nc.dram_tensor("x2ntm", [T + 1, D], BF16, kind="Internal").ap()
    NSLOT = 8 * NCAP + 2 * T + 128
    comb_h = nc.dram_tensor("comb_h", [NSLOT, 2], F32, kind="Internal").ap()

    x_r = x_fm.rearrange("(hc p) t -> p hc t", p=128)

    st = contextlib.ExitStack()
    sbuf = lambda n, s, d: st.enter_context(nc.sbuf_tensor(n, s, d))
    psum = lambda n, s, d: st.enter_context(nc.psum_tensor(n, s, d))

    with st:
        wgt_s = sbuf("wgt_s", [128, HC, E], BF16)
        cst_s = sbuf("cst_s", [128, 640], F32)
        cstb_s = sbuf("cstb_s", [128, 130], BF16)
        identb = sbuf("identb", [128, 128], BF16)
        gmb_s = sbuf("gmb_s", [128, 2, HC], F32)
        ba_s = sbuf("ba_s", [128, NP], F32)
        ltms = sbuf("ltms", [128, BFD, E], F32)
        swg = sbuf("swg", [8, 1], F32)

        ident = cst_s[:, 0:128]
        mask_c = cst_s[:, 128:256]
        onesb = cstb_s[:, 0:128]
        bd2 = cstb_s[:, 128:130]
        epsc = cst_s[:, 514:515]
        c24 = cst_s[:, 515:516]

        pall = [psum(f"p{i}", [128, 512], F32) for i in range(8)]
        p0, p1, p2, p3, p4, p5, p6, p7 = pall

        sems = {n: st.enter_context(nc.semaphore(n))
                for n in ("dX", "dL", "dG", "dP0", "dP1", "dR0", "dR1", "dA", "dW2", "dB0", "dB1", "dB2", "dB3", "eT", "eV", "eA", "eG")}
        ct = {n: 0 for n in sems}

        blk = st.enter_context(nc.Block())

        ENG_SEM = {"tensor": "eT", "vector": "eV", "scalar": "eA",
                   "gpsimd": "eG", "sync": None}

        _NOAUTO = {"wait_ge", "wait_op", "nop", "drain", "sem_inc"}

        class _Rec:

            def __init__(self, eng, engname, auto):
                self._e = eng
                self._sem = ENG_SEM[engname]
                self._auto = auto and self._sem is not None
                self.last = None

            def bar(self):
                if self._auto:
                    self._e.wait_ge(sems[self._sem], ct[self._sem])
                elif self.last is not None:
                    sn = self._sem
                    self.last.then_inc(sems[sn], 1)
                    ct[sn] += 1
                    self._e.wait_ge(sems[sn], ct[sn])
                    self.last = None

            def __getattr__(self, name):
                a = getattr(self._e, name)
                if not callable(a):
                    return a

                def call(*args, **kw):
                    r = a(*args, **kw)
                    if hasattr(r, "then_inc") and name not in _NOAUTO:
                        self.last = r
                        if self._auto:
                            r.then_inc(sems[self._sem], 1)
                            ct[self._sem] += 1
                    return r
                return call

        bufs = {}

        def _be(b):
            return bufs.setdefault(b, {"w": None, "r": []})

        def stage(engname, body, reads=(), writes=(), waits=(), inc=None,
                  dma=None):
            wl = [(n, (ct[n] if v is None else v)) for (n, v) in waits]
            for b in reads:
                e = _be(b)
                if e["w"]:
                    wl.append(e["w"])
            for b in writes:
                e = _be(b)
                if e["w"]:
                    wl.append(e["w"])
                wl.extend(e["r"])
            auto = (inc is not None) and engname != "tensor" and dma is None
            if auto:
                own = ENG_SEM[engname]
                wl.append((own, ct[own]))
            if dma is not None:
                wl.append((dma[0], ct[dma[0]]))
            wd = {}
            for n, v in wl:
                if v > 0:
                    wd[n] = max(wd.get(n, 0), v)
            wv = [(sems[n], v) for n, v in wd.items()]

            def f(eng):
                for s, v in wv:
                    eng.wait_ge(s, v)
                rec = _Rec(eng, engname, auto)
                body(rec)
                if inc is not None and not auto:
                    tgt = rec.last if rec.last is not None else eng.nop()
                    tgt.then_inc(sems[inc], 1)
                    ct[inc] += 1
            getattr(blk, engname)(f)
            if dma is not None:
                ct[dma[0]] += dma[1]
                comp = (dma[0], ct[dma[0]])
            elif auto:
                comp = (ENG_SEM[engname], ct[ENG_SEM[engname]])
            elif inc is not None:
                comp = (inc, ct[inc])
            else:
                comp = None
            if comp is not None:
                ws = set(writes)
                for b in reads:
                    if b not in ws:
                        _be(b)["r"].append(comp)
                for b in writes:
                    bufs[b] = {"w": comp, "r": []}
            return dict(ct)

        cur = lambda n: ct[n]
        W = lambda *names: [(n, None) for n in names]

        def ld0(eng):
            eng.dma_start(out=cst_s[:], in_=cst[:, 0:640]).then_inc(sems["dX"], 16)
            eng.dma_start(out=gmb_s[:], in_=gmb[:]).then_inc(sems["dX"], 16)
            eng.dma_start(out=ba_s[:], in_=ba[:]).then_inc(sems["dX"], 16)
        stage("sync", ld0, writes=("cst",), dma=("dX", 48))

        with contextlib.ExitStack() as sA:
            sb1 = lambda n, s, d: sA.enter_context(nc.sbuf_tensor(n, s, d))
            wq_s = sb1("wq_s", [128, HC, DI], BF16)
            wk_s = sb1("wk_s", [128, HC, DI], BF16)
            wv_s = sb1("wv_s", [128, HC, DI], BF16)
            wa_s = sb1("wa_s", [128, HC, DI], BF16)
            wo_s = sb1("wo_s", [128, NP, D], BF16)
            nba = sb1("nba", [128, NP], F32)
            xb2 = [sb1(f"xbuf{j}", [128, HC, TT], BF16) for j in range(2)]
            t0b2 = [sb1(f"t0buf{j}", [128, HC, TT], BF16) for j in range(2)]
            sqc = sb1("sqc", [128, HC, TT], BF16)
            xn = sb1("xn", [128, HC, TT], BF16)
            sc0 = sb1("sc0", [128, NP, TT], F32)
            mu1 = sb1("mu1", [128, TT], F32)
            r1 = sb1("r1", [128, TT], F32)
            r2f = sb1("r2f", [128, TT], F32)
            mu3 = sb1("mu3", [128, TT], F32)
            r3 = sb1("r3", [128, TT], F32)
            q_s = sb1("q_s", [128, NP, TT], BF16)
            k_s = sb1("k_s", [128, NP, TT], BF16)
            al_s = sb1("al_s", [128, NP, TT], F32)
            v2 = [sb1(f"vbuf{j}", [128, NCH, DI], BF16) for j in range(2)]
            o_s = sb1("o_s", [128, NP, TT], BF16)
            e1_s = sb1("e1_s", [128, NP, TT], BF16)
            e2_s = sb1("e2_s", [128, NP, TT], BF16)
            e1l2 = [sb1(f"e1l{j}", [128, NP, NCH], F32) for j in range(2)]
            qt2 = [sb1(f"qt{j}", [128, NP, TT], BF16) for j in range(2)]
            ktb2 = [sb1(f"ktb{j}", [128, NP, TT], BF16) for j in range(2)]
            khb2 = [sb1(f"khb{j}", [128, NP, TT], BF16) for j in range(2)]
            rlnp = sb1("rlnp", [2, NP, TT], F32)
            rnt2 = [sb1(f"rnt{j}", [128, NCH, NP * 2], F32) for j in range(2)]
            pt_s = sb1("pt_s", [128, NH, C], BF16)
            khtm = sb1("khtm", [128, NH, DH], BF16)
            s32 = sb1("s32", [128, NP, DH], F32)
            sbf = sb1("sbf", [128, NP, DH], BF16)
            xtm = sb1("xtm", [128, D], F32)
            xntm = sb1("xntm", [128, D], BF16)
            mre = sb1("mre", [128, 2], F32)
            lgs = sb1("lgs", [8, TT], F32)

            def ldw(eng):
                for dst, src in ((wq_s, wq), (wk_s, wk), (wv_s, wv),
                                 (wa_s, wa), (wo_s, wo), (wgt_s, wgt)):
                    eng.dma_start(out=dst[:], in_=src[:]).then_inc(sems["dG"], 16)
                eng.dma_start(out=cstb_s[:], in_=cst[:, 256:386]).then_inc(
                    sems["dG"], 16)
                eng.dma_start(out=identb[:], in_=cst[:, 0:128]).then_inc(
                    sems["dG"], 16)
            stage("gpsimd", ldw, reads=("cst",), writes=("wts",),
                  dma=("dG", 128))
            wload_done = cur("dG")

            def init(eng):
                eng.memset(s32[:], 0.0)
                eng.memset(sbf[:], 0.0)
                eng.memset(xntm[:], 0.0)
                eng.tensor_scalar(nba[:], ba_s[:], -1.0, None, ALU.mult)
            stage("vector", init, reads=("cst",),
                  writes=("s", "nba", "xntm"), inc="eV")

            def f_trash(eng):
                eng.dma_start(out=x2ntm[T:T + 1, :],
                              in_=xntm[0:1, :]).then_inc(sems["dG"], 16)
            stage("gpsimd", f_trash, reads=("xntm",), dma=("dG", 16))

            def f_swg(eng):
                for hc in range(HC):
                    eng.matmul(p2[0:8, 0:1], wgt_s[:, hc, :], onesb[:, 0:1],
                               start=(hc == 0), stop=(hc == HC - 1))
            stage("tensor", f_swg, reads=("wts",), writes=("p2",), inc="eT")
            stage("vector", lambda eng: eng.tensor_copy(swg[:], p2[0:8, 0:1]),
                  reads=("p2",), writes=("swg",), inc="eV")


            def plan_xl(i):
                par = i % 2
                tsl = slice(i * TT, (i + 1) * TT)

                def f_xl(eng, par=par, tsl=tsl):
                    eng.dma_start(out=xb2[par][:],
                                  in_=x_r[:, :, tsl]).then_inc(sems["dL"], 16)
                stage("gpsimd", f_xl, writes=(f"xb{par}",), dma=("dL", 16))

            def plan_F(i, part):
                par = i % 2
                xb = xb2[par]
                t0b = t0b2[par]
                XB = f"xb{par}"
                T0 = f"t0b{par}"
                if part == 0:
                    stage("scalar",
                          lambda eng: eng.activation(sqc[:], xb[:], AF.Square),
                          reads=(XB,), writes=("sqc",), inc="eA")

                    def f_st1(eng):
                        for hc in range(HC):
                            eng.matmul(p0[:], onesb, xb[:, hc, :],
                                       start=(hc == 0), stop=(hc == HC - 1))
                        for hc in range(HC):
                            eng.matmul(p1[:], onesb, sqc[:, hc, :],
                                       start=(hc == 0), stop=(hc == HC - 1))
                    stage("tensor", f_st1, reads=(XB, "sqc", "wts"),
                          writes=("p0", "p1"), inc="eT")

                    def f_mv1(eng):
                        eng.tensor_scalar(mu1[:], p0[:], 1.0 / D, None, ALU.mult)
                        eng.tensor_scalar(r1[:], p1[:], 1.0 / D, None, ALU.mult)
                        eng.bar()
                        eng.tensor_tensor(sc0[:, 0, :], mu1[:], mu1[:], ALU.mult)
                        eng.bar()
                        eng.tensor_tensor(r1[:], r1[:], sc0[:, 0, :],
                                          ALU.subtract)
                    stage("vector", f_mv1, reads=("p0", "p1"),
                          writes=("mu1", "r1", "sc0"), inc="eV")

                    def f_rst1(eng):
                        eng.activation(r1[:], r1[:], AF.Ln, bias=epsc)
                        eng.bar()
                        eng.activation(r1[:], r1[:], AF.Exp, scale=-0.5)
                    stage("scalar", f_rst1, reads=("cst",),
                          writes=("r1",), inc="eA")

                    def f_ap1(eng):
                        for g in range(2):
                            for h4 in range(NP):
                                hc = g * NP + h4
                                eng.tensor_tensor(sc0[:, h4, :], xb[:, hc, :],
                                                  mu1[:], ALU.subtract)
                            eng.bar()
                            for h4 in range(NP):
                                hc = g * NP + h4
                                eng.tensor_tensor(t0b[:, hc, :], sc0[:, h4, :],
                                                  r1[:], ALU.mult)
                            eng.bar()
                    stage("vector", f_ap1, reads=(XB, "mu1", "r1"),
                          writes=("sc0", T0), inc="eV")
                elif part == 1:
                    def f_r2v(eng):
                        eng.tensor_tensor(sc0[:, 2, :], r1[:], r1[:], ALU.mult)
                        eng.bar()
                        eng.tensor_scalar(sc0[:, 2, :], sc0[:, 2, :],
                                          -LN_EPS, 1.0, ALU.mult, ALU.add)
                    stage("vector", f_r2v, reads=("r1",), writes=("sc0",),
                          inc="eV")

                    def f_rst2(eng):
                        eng.activation(r2f[:], sc0[:, 2, :], AF.Ln, bias=epsc)
                        eng.bar()
                        eng.activation(r2f[:], r2f[:], AF.Exp, scale=-0.5)
                    stage("scalar", f_rst2, reads=("cst", "sc0"),
                          writes=("r2f",), inc="eA")

                    def f_ap2(eng):
                        for g in range(2):
                            for h4 in range(NP):
                                hc = g * NP + h4
                                eng.tensor_tensor(sc0[:, h4, :], t0b[:, hc, :],
                                                  r2f[:], ALU.mult)
                            eng.bar()
                            for h4 in range(NP):
                                hc = g * NP + h4
                                eng.tensor_scalar(xn[:, hc, :], sc0[:, h4, :],
                                                  gmb_s[:, 0, hc:hc + 1],
                                                  gmb_s[:, 1, hc:hc + 1],
                                                  ALU.mult, ALU.add)
                            eng.bar()
                    stage("vector", f_ap2, reads=(T0, "r2f", "cst"),
                          writes=("sc0", "xn"), inc="eV")
                elif part == 2:
                    for wsb, dname, kind in ((wq_s, "q", "v"), (wk_s, "k", "v"),
                                             (wa_s, "al", "a")):
                        dst = {"q": q_s, "k": k_s, "al": al_s}[dname]
                        for icp in range(NP):
                            bank = pall[icp % 2]
                            bn = f"p{icp % 2}"

                            def f_t(eng, wsb=wsb, icp=icp, bank=bank):
                                for hc in range(HC):
                                    eng.matmul(bank[:],
                                               wsb[:, hc, icp * 128:(icp + 1) * 128],
                                               xn[:, hc, :],
                                               start=(hc == 0), stop=(hc == HC - 1))
                            stage("tensor", f_t, reads=("xn", "wts"),
                                  writes=(bn,), inc="eT")
                            if kind == "v":
                                stage("vector",
                                      lambda eng, dst=dst, icp=icp, bank=bank:
                                      eng.tensor_copy(dst[:, icp, :], bank[:]),
                                      reads=(bn,), writes=(dname,), inc="eV")
                            else:
                                def f_sp(eng, icp=icp, bank=bank):
                                    eng.activation(sc0[:, icp, :], bank[:],
                                                   AF.Exp, scale=-1.0,
                                                   bias=nba[:, icp:icp + 1])
                                    eng.bar()
                                    eng.activation(al_s[:, icp, :],
                                                   sc0[:, icp, :],
                                                   AF.Ln, bias=1.0)
                                stage("scalar", f_sp, reads=(bn, "nba"),
                                      writes=("sc0", "al"), inc="eA")
                    for tc in range(NCH):
                        bank = pall[tc % 2]
                        bn = f"p{tc % 2}"

                        def f_tv(eng, tc=tc, bank=bank):
                            for hc in range(HC):
                                eng.matmul(bank[:], xn[:, hc, tc * C:(tc + 1) * C],
                                           wv_s[:, hc, :],
                                           start=(hc == 0), stop=(hc == HC - 1))
                        stage("tensor", f_tv, reads=("xn", "wts"),
                              writes=(bn,), inc="eT")
                        stage("vector",
                              lambda eng, tc=tc, bank=bank, par=par:
                              eng.tensor_copy(v2[par][:, tc, :], bank[:]),
                              reads=(bn,), writes=(f"v{par}",), inc="eV")
                elif part == 3:
                    def g_scan(eng):
                        for pc in range(NP):
                            for lc in range(NCH):
                                csl = slice(lc * C, (lc + 1) * C)
                                eng.tensor_tensor_scan(sc0[:, pc, csl],
                                                       al_s[:, pc, csl],
                                                       al_s[:, pc, csl], 0.0,
                                                       ALU.add, ALU.bypass)
                    stage("vector", g_scan, reads=("al",), writes=("sc0",),
                          inc="eV")

                    def g_exp(eng, par=par):
                        eng.activation(e1_s[:], sc0[:], AF.Exp, scale=-1.0)
                        eng.activation(e2_s[:], sc0[:], AF.Exp)
                        eng.activation(e1l2[par][:], sc0[:, :, C - 1::C],
                                       AF.Exp, scale=-1.0)
                    stage("scalar", g_exp, reads=("sc0",),
                          writes=("e1", "e2", f"e1l{par}"), inc="eA")

                    stage("scalar",
                          lambda eng: eng.activation(sqc[:, 0:NP, :], k_s[:],
                                                     AF.Square),
                          reads=("k",), writes=("sqc",), inc="eA")

                    def g_ss(eng):
                        for pc in range(NP):
                            eng.matmul(p0[32 * pc:32 * pc + 2, 0:TT], bd2,
                                       sqc[:, pc, :], start=True, stop=True,
                                       tile_position=(0, 32 * pc))
                    stage("tensor", g_ss, reads=("sqc", "wts"),
                          writes=("p0",), inc="eT")

                    def g_rln(eng):
                        for pc in range(NP):
                            eng.tensor_copy(rlnp[:, pc, :],
                                            p0[32 * pc:32 * pc + 2, 0:TT])
                    stage("vector", g_rln, reads=("p0",), writes=("rlnp",),
                          inc="eV")

                    def g_rle(eng):
                        eng.activation(rlnp[:], rlnp[:], AF.Ln, bias=c24[0:2, :])
                        eng.bar()
                        eng.activation(rlnp[:], rlnp[:], AF.Exp, scale=-0.5)
                    stage("scalar", g_rle, reads=("cst",), writes=("rlnp",),
                          inc="eA")
                elif part == 4:
                    def g_rlt(eng):
                        for pc in range(NP):
                            for lc in range(NCH):
                                col = (pc * NCH + lc) * 2
                                eng.transpose(p1[:, col:col + 2],
                                              rlnp[:, pc, lc * C:(lc + 1) * C],
                                              ident[0:2, 0:2])
                    stage("tensor", g_rlt, reads=("rlnp", "cst"),
                          writes=("p1",), inc="eT")

                    def g_rnte(eng, par=par):
                        for pc in range(NP):
                            eng.tensor_copy(
                                rnt2[par][:, :, 2 * pc:2 * pc + 2],
                                p1[:, pc * 2 * NCH:(pc + 1) * 2 * NCH]
                                .rearrange("p (t j) -> p t j", j=2))
                    stage("vector", g_rnte, reads=("p1",),
                          writes=(f"rnt{par}",), inc="eV")

                    def g_mul(eng, par=par):
                        eng.tensor_tensor(qt2[par][:], q_s[:], e1_s[:], ALU.mult)
                        eng.tensor_tensor(ktb2[par][:], k_s[:], e2_s[:], ALU.mult)
                        eng.bar()
                        for pc in range(NP):
                            for lc in range(NCH):
                                csl = slice(lc * C, (lc + 1) * C)
                                eng.tensor_scalar(khb2[par][:, pc, csl],
                                                  ktb2[par][:, pc, csl],
                                                  e1l2[par][:, pc, lc:lc + 1],
                                                  None, ALU.mult)
                    stage("vector", g_mul,
                          reads=("q", "k", "e1", "e2", f"e1l{par}"),
                          writes=(f"qt{par}", f"ktb{par}", f"khb{par}"),
                          inc="eV")

            def plan_B(i, part):
                par = i % 2
                xb = xb2[par]
                t0b = t0b2[par]
                qt = qt2[par]
                kt = ktb2[par]
                kh = khb2[par]
                e1l = e1l2[par]
                rnt = rnt2[par]
                v_s = v2[par]
                XB = f"xb{par}"
                T0 = f"t0b{par}"
                QT = f"qt{par}"
                KT = f"ktb{par}"
                KH = f"khb{par}"
                EL = f"e1l{par}"
                RN = f"rnt{par}"
                VS = f"v{par}"
                if part in (0, 1):
                    for lc in ((0, 1) if part == 0 else (2, 3)):
                        csl = slice(lc * C, (lc + 1) * C)

                        def g_t2(eng, csl=csl, qt=qt, kt=kt, kh=kh):
                            for pc in range(NP):
                                for j in range(2):
                                    bank = p2 if j == 0 else p3
                                    eng.matmul(bank[:, pc * C:(pc + 1) * C],
                                               kt[j * 64:(j + 1) * 64, pc, csl],
                                               qt[j * 64:(j + 1) * 64, pc, csl],
                                               start=True, stop=True)
                            for pc in range(NP):
                                eng.matmul(p6[:, pc * C:(pc + 1) * C],
                                           kh[:, pc, csl], identb[:],
                                           start=True, stop=True)
                        stage("tensor", g_t2, reads=(QT, KT, KH, "wts"),
                              writes=("p2", "p3", "p6"), inc="eT")

                        def g_v4(eng, lc=lc, rnt=rnt):
                            for pc in range(NP):
                                for j in range(2):
                                    h = 2 * pc + j
                                    bank = p2 if j == 0 else p3
                                    eng.scalar_tensor_tensor(
                                        pt_s[:, h, :],
                                        bank[:, pc * C:(pc + 1) * C],
                                        rnt[:, lc, pc * 2 + j:pc * 2 + j + 1],
                                        mask_c, ALU.mult, ALU.mult)
                            for pc in range(NP):
                                for j in range(2):
                                    h = 2 * pc + j
                                    eng.tensor_scalar(
                                        khtm[:, h, :],
                                        p6[:, pc * C + j * 64:pc * C + (j + 1) * 64],
                                        rnt[:, lc, pc * 2 + j:pc * 2 + j + 1],
                                        None, ALU.mult)
                        stage("vector", g_v4,
                              reads=("p2", "p3", "p6", RN, "cst"),
                              writes=("pt", "khtm"), inc="eV")

                        def g_t3(eng, lc=lc, csl=csl, qt=qt, v_s=v_s):
                            for pc in range(NP):
                                for j in range(2):
                                    h = 2 * pc + j
                                    eng.matmul(
                                        p4[j * 64:(j + 1) * 64, pc * C:(pc + 1) * C],
                                        v_s[:, lc, h * 64:(h + 1) * 64],
                                        pt_s[:, h, :],
                                        start=True, stop=False,
                                        tile_position=(0, 64 * j))
                                    eng.matmul(
                                        p4[j * 64:(j + 1) * 64, pc * C:(pc + 1) * C],
                                        sbf[j * 64:(j + 1) * 64, pc, :],
                                        qt[j * 64:(j + 1) * 64, pc, csl],
                                        start=False, stop=True,
                                        tile_position=(64 * j, 64 * j))
                            for pc in range(NP):
                                for j in range(2):
                                    h = 2 * pc + j
                                    eng.matmul(
                                        p5[j * 64:(j + 1) * 64, pc * 64:(pc + 1) * 64],
                                        khtm[:, h, :],
                                        v_s[:, lc, h * 64:(h + 1) * 64],
                                        start=True, stop=True,
                                        tile_position=(0, 64 * j))
                        stage("tensor", g_t3, reads=(VS, "pt", "khtm", "s", QT),
                              writes=("p4", "p5"), inc="eT")

                        def g_v5(eng, lc=lc, e1l=e1l):
                            for pc in range(NP):
                                eng.tensor_copy(o_s[:, pc, lc * C:(lc + 1) * C],
                                                p4[:, pc * C:(pc + 1) * C])
                            for pc in range(NP):
                                eng.tensor_scalar(s32[:, pc, :], s32[:, pc, :],
                                                  e1l[:, pc, lc:lc + 1], None,
                                                  ALU.mult)
                            eng.bar()
                            eng.tensor_tensor(
                                s32.ap().rearrange("p a b -> p (a b)"),
                                s32.ap().rearrange("p a b -> p (a b)"),
                                p5[:, 0:NP * DH], ALU.add)
                            eng.bar()
                            eng.tensor_copy(sbf[:], s32[:])
                        stage("vector", g_v5, reads=("p4", "p5", EL),
                              writes=("o", "s"), inc="eV")
                elif part == 2:
                    def f_x2p(eng, t0b=t0b, xb=xb):
                        eng.tensor_tensor(
                            t0b.ap().rearrange("p a b -> p (a b)"),
                            t0b.ap().rearrange("p a b -> p (a b)"),
                            xb.ap().rearrange("p a b -> p (a b)"), ALU.add)
                    stage("vector", f_x2p, reads=(XB,), writes=(T0,), inc="eV")
                    for hc in range(HC):
                        bank = pall[(2, 3, 6, 7)[hc % 4]]
                        bn = f"p{(2, 3, 6, 7)[hc % 4]}"

                        def f_op(eng, hc=hc, bank=bank):
                            for icp in range(NP):
                                eng.matmul(bank[:],
                                           wo_s[:, icp, hc * 128:(hc + 1) * 128],
                                           o_s[:, icp, :],
                                           start=(icp == 0), stop=(icp == NP - 1))
                        stage("tensor", f_op, reads=("o", "wts"),
                              writes=(bn,), inc="eT")

                        def f_x2a(eng, hc=hc, bank=bank, xb=xb, t0b=t0b):
                            eng.tensor_tensor(xb[:, hc, :], bank[:],
                                              t0b[:, hc, :], ALU.add)
                        stage("vector", f_x2a, reads=(bn, T0), writes=(XB,),
                              inc="eV")
                elif part == 3:
                    stage("scalar",
                          lambda eng, xb=xb: eng.activation(sqc[:], xb[:],
                                                            AF.Square),
                          reads=(XB,), writes=("sqc",), inc="eA")

                    def f_st3(eng, xb=xb):
                        for hc in range(HC):
                            eng.matmul(p6[:], onesb, xb[:, hc, :],
                                       start=(hc == 0), stop=(hc == HC - 1))
                        for hc in range(HC):
                            eng.matmul(p7[:], onesb, sqc[:, hc, :],
                                       start=(hc == 0), stop=(hc == HC - 1))
                    stage("tensor", f_st3, reads=(XB, "sqc", "wts"),
                          writes=("p6", "p7"), inc="eT")

                    def f_mv3(eng):
                        eng.tensor_scalar(mu3[:], p6[:], 1.0 / D, None, ALU.mult)
                        eng.tensor_scalar(r3[:], p7[:], 1.0 / D, None, ALU.mult)
                        eng.bar()
                        eng.tensor_tensor(xtm[:, 0:TT], mu3[:], mu3[:], ALU.mult)
                        eng.bar()
                        eng.tensor_tensor(r3[:], r3[:], xtm[:, 0:TT], ALU.subtract)
                    stage("vector", f_mv3, reads=("p6", "p7"),
                          writes=("mu3", "r3", "xtm"), inc="eV")

                    def f_rst3(eng):
                        eng.activation(r3[:], r3[:], AF.Ln, bias=epsc)
                        eng.bar()
                        eng.activation(r3[:], r3[:], AF.Exp, scale=-0.5)
                    stage("scalar", f_rst3, reads=("cst",), writes=("r3",),
                          inc="eA")

                    for tc in range(NCH):
                        def f_tr(eng, tc=tc, xb=xb):
                            for hc in range(HC):
                                bank = p2 if hc < 4 else p3
                                eng.matmul(
                                    bank[:, (hc % 4) * 128:(hc % 4 + 1) * 128],
                                    xb[:, hc, tc * C:(tc + 1) * C], identb[:],
                                    start=True, stop=True)
                            eng.transpose(p4[:, 0:1],
                                          mu3[0:1, tc * C:(tc + 1) * C],
                                          ident[0:1, 0:1])
                            eng.transpose(p4[:, 1:2],
                                          r3[0:1, tc * C:(tc + 1) * C],
                                          ident[0:1, 0:1])
                        stage("tensor", f_tr,
                              reads=(XB, "mu3", "r3", "wts", "cst"),
                              writes=("p2", "p3", "p4"), inc="eT")

                        def f_tre(eng, tc=tc):
                            eng.tensor_copy(xtm[:, 0:512], p2[:, 0:512])
                            eng.tensor_copy(xtm[:, 512:1024], p3[:, 0:512])
                            eng.tensor_copy(mre[:], p4[:, 0:2])
                            eng.bar()
                            eng.tensor_scalar(xntm[:], xtm[:], mre[:, 0:1],
                                              mre[:, 1:2], ALU.subtract, ALU.mult)
                        stage("vector", f_tre, reads=("p2", "p3", "p4"),
                              writes=("xtm", "xntm", "mre"), inc="eV")

                        def f_tro(eng, tc=tc, i=i):
                            r0 = i * TT + tc * C
                            eng.dma_start(out=out[r0:r0 + 128, :],
                                          in_=xtm[:]).then_inc(sems["dG"], 16)
                            eng.dma_start(out=x2ntm[r0:r0 + 128, :],
                                          in_=xntm[:]).then_inc(sems["dG"], 16)
                        stage("gpsimd", f_tro, reads=("xtm", "xntm"),
                              dma=("dG", 32))
                elif part == 4:
                    def f_lg(eng, xb=xb):
                        for hc in range(HC):
                            eng.matmul(p5[0:8, 0:TT], wgt_s[:, hc, :],
                                       xb[:, hc, :],
                                       start=(hc == 0), stop=(hc == HC - 1))
                    stage("tensor", f_lg, reads=(XB, "wts"), writes=("p5",),
                          inc="eT")

                    def f_lge(eng):
                        eng.tensor_scalar(lgs[:], mu3[0:8, :], swg[0:8, 0:1],
                                          None, ALU.mult)
                        eng.bar()
                        eng.tensor_tensor(lgs[:], p5[0:8, 0:TT], lgs[:],
                                          ALU.subtract)
                        eng.bar()
                        eng.tensor_tensor(lgs[:], lgs[:], r3[0:8, :], ALU.mult)
                    stage("vector", f_lge, reads=("p5", "mu3", "r3", "swg"),
                          writes=("lgs",), inc="eV")

                    def f_lgt(eng):
                        for tc in range(NCH):
                            eng.transpose(p4[:, 8 * tc:8 * tc + 8],
                                          lgs[0:8, tc * C:(tc + 1) * C],
                                          ident[0:8, 0:8])
                    stage("tensor", f_lgt, reads=("lgs", "cst"),
                          writes=("p4",), inc="eT")
                    stage("vector",
                          lambda eng, i=i:
                          eng.tensor_copy(ltms[:, i * NCH:(i + 1) * NCH, :],
                                          p4[:, 0:8 * NCH]
                                          .rearrange("p (a b) -> p a b", b=E)),
                          reads=("p4",), writes=("ltms",), inc="eV")

            for i in range(NTT + 1):
                fe = i < NTT
                be = i >= 1
                if fe:
                    plan_xl(i)
                if be:
                    plan_B(i - 1, 0)
                if fe:
                    plan_F(i, 0)
                if be:
                    plan_B(i - 1, 1)
                if fe:
                    plan_F(i, 1)
                if be:
                    plan_B(i - 1, 2)
                if fe:
                    plan_F(i, 2)
                if be:
                    plan_B(i - 1, 3)
                if fe:
                    plan_F(i, 3)
                    plan_F(i, 4)
                if be:
                    plan_B(i - 1, 4)

        with contextlib.ExitStack() as sR:
            sbr = lambda n, s, d: sR.enter_context(nc.sbuf_tensor(n, s, d))
            mfm = sbr("mfm", [8, T], F32)
            maxv = sbr("maxv", [128, BFD, 8], F32)
            argt = sbr("argt", [128, BFD, 8], mybir.dt.uint32)
            d12 = sbr("d12", [128, BFD], F32)
            w1v = sbr("w1v", [128, BFD], F32)
            w2v = sbr("w2v", [128, BFD], F32)
            wtm = sbr("wtm", [128, BFD, 8], F32)
            selA = sbr("selA", [128, BFD, 8], F32)
            selB = sbr("selB", [128, BFD, 8], F32)
            tok_s = sbr("tok_s", [128, BFD], F32)
            pfx = sbr("pfx", [8, T], F32)
            ptm = sbr("ptm", [128, BFD, 8], F32)
            sl1 = sbr("sl1", [128, BFD], F32)
            sl2 = sbr("sl2", [128, BFD], F32)
            sli1 = sbr("sli1", [128, BFD], mybir.dt.int32)
            sli2 = sbr("sli2", [128, BFD], mybir.dt.int32)
            qr1 = sbr("qr1", [128, BFD], F32)
            qr2 = sbr("qr2", [128, BFD], F32)
            fill_s = sbr("fill_s", [128, 2 * ((NSLOT + 127) // 128)], F32)
            tg1 = sbr("tg1", [128, BFD, 2], F32)
            tg2 = sbr("tg2", [128, BFD, 2], F32)

            def f_tok(eng):
                eng.dma_start(out=tok_s[:], in_=tokc[:]).then_inc(
                    sems["dX"], 16)
            stage("sync", f_tok,
                  waits=W("eV", "eA", "eT", "eG", "dG"), dma=("dX", 16))
            tok_done = cur("dX")

            stage("vector", lambda eng: eng.memset(d12[0:1, 0:1], 0.0),
                  waits=W("eA", "eT", "eG") + [("dX", tok_done)], inc="eV")
            stage("scalar",
                  lambda eng: eng.copy(d12[0:1, 1:2], d12[0:1, 0:1]),
                  waits=W("eV", "eT", "eG"), inc="eA")
            stage("gpsimd", lambda eng: eng.memset(d12[0:1, 2:3], 0.0),
                  waits=W("eV", "eA", "eT"), inc="eG")

            def f_rmax(eng):
                for bi in range(BFD):
                    eng.max(out=maxv[:, bi, :], in_=ltms[:, bi, :])
                eng.bar()
                for bi in range(BFD):
                    eng.max_index(out=argt[:, bi, :], in_max=maxv[:, bi, :],
                                  in_values=ltms[:, bi, :])
                eng.tensor_tensor(d12[:], maxv[:, :, 0], maxv[:, :, 1],
                                  ALU.subtract)
            stage("vector", f_rmax, waits=W("eA", "eG"), inc="eV")

            def f_rsig(eng):
                eng.activation(w1v[:], d12[:], AF.Sigmoid)
                eng.activation(w2v[:], d12[:], AF.Sigmoid, scale=-1.0)
            stage("scalar", f_rsig, waits=W("eV"), inc="eA")

            def f_rw(eng):
                for e in range(E):
                    eng.tensor_scalar(selA[:, :, e], argt[:, :, 0], float(e),
                                      None, ALU.is_equal)
                    eng.tensor_scalar(selB[:, :, e], argt[:, :, 1], float(e),
                                      None, ALU.is_equal)
                eng.bar()
                eng.tensor_tensor(wtm.ap().rearrange("p a b -> p (a b)"),
                                  selA.ap().rearrange("p a b -> p (a b)"),
                                  selB.ap().rearrange("p a b -> p (a b)"),
                                  ALU.add)
            stage("vector", f_rw, waits=W("eA"), inc="eV")

            for bi in range(BFD):
                bank = pall[2 + (bi % 2)]
                stage("tensor",
                      lambda eng, bi=bi, bank=bank:
                      eng.transpose(bank[0:8, 0:128], wtm[:, bi, :], ident),
                      waits=W("eV"), inc="eT")
                stage("vector",
                      lambda eng, bi=bi, bank=bank:
                      eng.tensor_copy(mfm[:, bi::BFD], bank[0:8, 0:128]),
                      waits=W("eT"), inc="eV")

            def f_pfx(eng):
                eng.tensor_tensor_scan(pfx[:], mfm[:], mfm[:], 0.0,
                                       ALU.add, ALU.bypass)
            stage("vector", f_pfx, waits=W("eV"), inc="eV")

            for bi in range(BFD):
                bank = pall[2 + (bi % 2)]
                stage("tensor",
                      lambda eng, bi=bi, bank=bank:
                      eng.transpose(bank[:, 0:8], pfx[0:8, bi::BFD],
                                    ident[0:8, 0:8]),
                      waits=W("eV"), inc="eT")
                stage("vector",
                      lambda eng, bi=bi, bank=bank:
                      eng.tensor_copy(ptm[:, bi, :], bank[:, 0:8]),
                      waits=W("eT"), inc="eV")

            TRASH = 8 * NCAP
            AX = mybir.AxisListType.X

            def f_slot(eng):
                for e in range(E):
                    eng.scalar_tensor_tensor(wtm[:, :, e], ptm[:, :, e],
                                             float(e * NCAP - 1),
                                             selA[:, :, e], ALU.add, ALU.mult)
                eng.bar()
                eng.tensor_reduce(sl1[:], wtm.ap(), AX, ALU.add)
                eng.bar()
                for e in range(E):
                    eng.scalar_tensor_tensor(wtm[:, :, e], ptm[:, :, e],
                                             float(e * NCAP - 1),
                                             selB[:, :, e], ALU.add, ALU.mult)
                eng.bar()
                eng.tensor_reduce(sl2[:], wtm.ap(), AX, ALU.add)
                eng.bar()
                for e in range(E):
                    eng.scalar_tensor_tensor(wtm[:, :, e], ptm[:, :, e],
                                             -1.0, selA[:, :, e],
                                             ALU.add, ALU.mult)
                eng.bar()
                eng.tensor_reduce(qr1[:], wtm.ap(), AX, ALU.add)
                eng.bar()
                for e in range(E):
                    eng.scalar_tensor_tensor(wtm[:, :, e], ptm[:, :, e],
                                             -1.0, selB[:, :, e],
                                             ALU.add, ALU.mult)
                eng.bar()
                eng.tensor_reduce(qr2[:], wtm.ap(), AX, ALU.add)
                eng.bar()
                eng.tensor_scalar(d12[:], qr1[:], float(NCAP), None, ALU.is_ge)
                eng.bar()
                eng.scalar_tensor_tensor(sl1[:], d12[:], float(TRASH),
                                         sl1[:], ALU.mult, ALU.max)
                eng.bar()
                eng.tensor_scalar(d12[:], qr2[:], float(NCAP), None, ALU.is_ge)
                eng.bar()
                eng.scalar_tensor_tensor(sl2[:], d12[:], float(TRASH),
                                         sl2[:], ALU.mult, ALU.max)
                eng.bar()
                eng.tensor_copy(sli1[:], sl1[:])
                eng.tensor_copy(sli2[:], sl2[:])
                eng.memset(fill_s[:, 0::2], float(T))
                eng.memset(fill_s[:, 1::2], 0.0)
                eng.tensor_copy(tg1[:, :, 0], tok_s[:])
                eng.tensor_copy(tg2[:, :, 0], tok_s[:])
                eng.tensor_copy(tg1[:, :, 1], w1v[:])
                eng.tensor_copy(tg2[:, :, 1], w2v[:])
            stage("vector", f_slot, waits=W("eV"), inc="eV")

            def f_fill(eng):
                eng.dma_start(out=comb_h.rearrange("(p c) v -> p (c v)",
                                                   p=128),
                              in_=fill_s[:]).then_inc(sems["dG"], 16)
            stage("gpsimd", f_fill, waits=W("eV"), dma=("dG", 16))

            from concourse.bass_core_rust import IndirectOffsetOnAxis as IOX
            def f_scat(eng):
                for bi in range(BFD):
                    eng.indirect_dma_start(
                        comb_h[:], IOX(sli1[:, bi:bi + 1], 0),
                        tg1[:, bi, :], None).then_inc(sems["dG"], 16)
                    eng.indirect_dma_start(
                        comb_h[:], IOX(sli2[:, bi:bi + 1], 0),
                        tg2[:, bi, :], None).then_inc(sems["dG"], 16)
            stage("gpsimd", f_scat, waits=W("dG", "eV"),
                  dma=("dG", BFD * 2 * 16))

            p1_end = dict(ct)


        if PHASE2 != "none":
            from concourse.bass_core_rust import IndirectOffsetOnAxis as IOX2
            NNT = NCAP // 512
            with contextlib.ExitStack() as sB:
                sb2 = lambda n, s, d: sB.enter_context(nc.sbuf_tensor(n, s, d))
                w13 = sb2("w13", [128, 2, HC, I_FF], BF16)
                w2s = sb2("w2s", [128, IC, D], BF16)
                xg = [sb2("xg0", [128, HC, NCAP], BF16),
                      sb2("xg1", [128, HC, NCAP], BF16)]
                g_s = sb2("g_s", [128, IC, 512], BF16)
                silu_s = sb2("silu_s", [128, 4, 512], BF16)
                gtm = [sb2(f"gtm{j}", [128, D], BF16) for j in range(4)]
                stg2 = [sb2("stg20", [128, D], F32), sb2("stg21", [128, D], F32)]
                comb_s = [sb2("comb0", [128, NB, 2], F32),
                          sb2("comb1", [128, NB, 2], F32)]
                gat_s = [sb2("gat0", [128, NB], F32), sb2("gat1", [128, NB], F32)]
                idxi_s = [sb2("idxi0", [128, NB], mybir.dt.int32),
                          sb2("idxi1", [128, NB], mybir.dt.int32)]

                w13_done = {}
                w2_done = {}
                w13_free = {}
                w2_free = {}
                rb_done = {}
                cast_done = {}
                gath_done = {}
                tr_done = {}
                h_done = {}
                evac_done = {}
                scat_done = {}
                ye_done = {}
                gv = {}

                def plan_w13(e, ic4, wait_eT):
                    bsn = "dB%d" % ic4

                    def f_w13(eng, e=e, ic4=ic4, bsn=bsn):
                        eng.dma_start(
                            out=w13[:, 0, :, ic4 * 512:(ic4 + 1) * 512],
                            in_=w1t[e, :, :, ic4 * 512:(ic4 + 1) * 512]
                        ).then_inc(sems[bsn], 16)
                        eng.dma_start(
                            out=w13[:, 1, :, ic4 * 512:(ic4 + 1) * 512],
                            in_=w3t[e, :, :, ic4 * 512:(ic4 + 1) * 512]
                        ).then_inc(sems[bsn], 16)
                    wts = [("eT", wait_eT)] if wait_eT else                         [("eV", p1_end["eV"]), ("eA", p1_end["eA"]),
                         ("eT", p1_end["eT"]), ("dG", p1_end["dG"])]
                    stage("gpsimd", f_w13, waits=wts, dma=(bsn, 32))
                    w13_done[(e, ic4)] = (bsn, cur(bsn))

                def plan_w2(e):
                    def f_w2(eng, e=e):
                        eng.dma_start(out=w2s[:], in_=w2t[e]).then_inc(
                            sems["dW2"], 16)
                    wts = [("eT", w2_free[e - 1])] if e > 0 else []
                    stage("gpsimd", f_w2, waits=wts, dma=("dW2", 16))
                    w2_done[e] = cur("dW2")

                def plan_dispatch(e):
                    par = e % 2
                    rsn = "dR%d" % par

                    def f_rb(eng, e=e, par=par, rsn=rsn):
                        eng.dma_start(
                            out=comb_s[par][:],
                            in_=comb_h[e * NCAP:(e + 1) * NCAP, :]
                            .rearrange("(p c) v -> p c v", c=NB)
                        ).then_inc(sems[rsn], 16)

                    rwts = [("dG", p1_end["dG"])]
                    if e >= 2:
                        rwts.append(("eV", ye_done[e - 2]))
                        rwts.append(("eV", cast_done[e - 2]))
                    stage("sync", f_rb, waits=rwts, dma=(rsn, 16))
                    rb_done[e] = (rsn, cur(rsn))

                    def f_cast(eng, par=par):
                        eng.tensor_copy(idxi_s[par][:], comb_s[par][:, :, 0])
                        eng.tensor_copy(gat_s[par][:], comb_s[par][:, :, 1])
                    cwts = [rb_done[e]]
                    if e >= 2:
                        cwts.append(("dA", scat_done[(e - 1) * NB - 1]))
                    stage("vector", f_cast, waits=cwts, inc="eV")
                    cast_done[e] = cur("eV")

                    for c in range(NB):
                        psn = "dP%d" % (c % 2)

                        def f_gth(eng, e=e, c=c, par=par, psn=psn):
                            eng.indirect_dma_start(
                                gtm[c % 4][:], None, x2ntm[:],
                                IOX2(idxi_s[par][:, c:c + 1], 0),
                            ).then_inc(sems[psn], 16)
                        gwts = [("eV", cast_done[e])]
                        prevc = (e, c - 4) if c >= 4 else (e - 1, NB - 4 + c)
                        if prevc in tr_done:
                            gwts.append(("eT", tr_done[prevc]))
                        stage("gpsimd", f_gth, waits=gwts, dma=(psn, 16))
                        gath_done[(e, c)] = (psn, cur(psn))

                        def f_trn(eng, c=c):
                            for hc in range(HC):
                                bank = p6 if hc < 4 else p7
                                eng.matmul(
                                    bank[:, (hc % 4) * 128:(hc % 4 + 1) * 128],
                                    gtm[c % 4][:, hc * 128:(hc + 1) * 128],
                                    identb[:], start=True, stop=True)
                        twts = [gath_done[(e, c)]]
                        if e >= 1 and c == 0:
                            twts.append(("eA", sl_nt2.get(e - 1, 0)))
                            twts.append(("eV", g_nt2.get(e - 1, 0)))
                        if c > 0:
                            twts.append(("eV", tevv[(e, c - 1)]))
                        if e >= 2:
                            twts.append(("eT", h_done[e - 2]))
                        stage("tensor", f_trn, waits=twts, inc="eT")
                        tr_done[(e, c)] = cur("eT")

                        def f_tev(eng, e=e, c=c, par=par):
                            eng.tensor_copy(
                                xg[par][:, 0:4, c * 128:(c + 1) * 128],
                                p6[:, 0:512].rearrange("p (a b) -> p a b",
                                                       b=128))
                            eng.tensor_copy(
                                xg[par][:, 4:8, c * 128:(c + 1) * 128],
                                p7[:, 0:512].rearrange("p (a b) -> p a b",
                                                       b=128))
                        stage("vector", f_tev, waits=[("eT", tr_done[(e, c)])],
                              inc="eV")
                        tevv[(e, c)] = cur("eV")
                    tev_done[e] = cur("eV")

                silu_last = {}
                sl_g = {}
                g_last = {}
                sl_nt2 = {}
                g_nt2 = {}
                tev_done = {}
                tevv = {}
                for _ic4 in range(4):
                    plan_w13(0, _ic4, None)
                plan_w2(0)
                plan_dispatch(0)

                for e in range(E):
                    par = e % 2
                    for nt in range(NNT):
                        ssl = slice(nt * 512, (nt + 1) * 512)

                        for ic4 in range(4):
                            for m in range(4):
                                isl = slice(ic4 * 512 + m * 128,
                                            ic4 * 512 + (m + 1) * 128)

                                b3 = 4 + (m % 2 if nt == NNT - 1 else m)

                                def f_h(eng, isl=isl, m=m, par=par, ssl=ssl,
                                        b3=b3):
                                    for hc in range(HC):
                                        eng.matmul(pall[m][:],
                                                   w13[:, 0, hc, isl],
                                                   xg[par][:, hc, ssl],
                                                   start=(hc == 0),
                                                   stop=(hc == HC - 1))
                                    for hc in range(HC):
                                        eng.matmul(pall[b3][:],
                                                   w13[:, 1, hc, isl],
                                                   xg[par][:, hc, ssl],
                                                   start=(hc == 0),
                                                   stop=(hc == HC - 1))
                                wl = [w13_done[(e, ic4)]]
                                if ic4 == 0 and nt == 0:
                                    wl.append(("eV", tev_done[e]))
                                prevk = ((ic4 - 1) % 4, m)
                                if prevk in gv:
                                    wl.append(("eV", gv[prevk]))
                                if prevk in sl_g:
                                    wl.append(("eA", sl_g[prevk]))
                                if nt == NNT - 1 and m >= 2:
                                    wl.append(("eV", gv[(ic4, m - 2)]))
                                if (nt == NNT - 1 and m < 2 and ic4 > 0
                                        and (ic4 - 1, m + 2) in gv):
                                    wl.append(("eV", gv[(ic4 - 1, m + 2)]))
                                if ic4 == 0:
                                    yk = ((e, nt - 1, 4 + m) if nt > 0
                                          else (e - 1, NNT - 1, 4 + m))
                                    if yk in ye_done:
                                        wl.append(("eV", ye_done[yk]))
                                stage("tensor", f_h, waits=wl, inc="eT")
                                hdone = cur("eT")
                                if nt == NNT - 1 and m == 3 and e + 1 < E:
                                    plan_w13(e + 1, ic4, hdone)

                                stage("scalar",
                                      lambda eng, m=m:
                                      eng.activation(silu_s[:, m, :],
                                                     pall[m][:], AF.Silu),
                                      waits=[("eT", hdone)], inc="eA")
                                sl_g[(ic4, m)] = cur("eA")

                                def f_g(eng, ic4=ic4, m=m, b3=b3):
                                    eng.tensor_tensor(g_s[:, ic4 * 4 + m, :],
                                                      silu_s[:, m, :],
                                                      pall[b3][:], ALU.mult)
                                stage("vector", f_g, waits=W("eA"), inc="eV")
                                gv[(ic4, m)] = cur("eV")

                        if nt == NNT - 2:
                            sl_nt2[e] = cur("eA")
                            g_nt2[e] = cur("eV")
                        if nt == NNT - 1:
                            h_done[e] = cur("eT")
                            w13_free[e] = cur("eT")
                            silu_last[e] = cur("eA")
                            g_last[e] = cur("eV")
                            if e + 1 < E:
                                plan_dispatch(e + 1)

                        for cg in range(4):
                            c = nt * 4 + cg
                            for half in range(2):
                                bank = pall[(2 * cg + half) % 4]
                                hsl = slice(half * 512, (half + 1) * 512)

                                def f_y(eng, cg=cg, half=half, bank=bank,
                                        hsl=hsl):
                                    for icc in range(IC):
                                        eng.matmul(bank[:],
                                                   g_s[:, icc,
                                                       cg * 128:(cg + 1) * 128],
                                                   w2s[:, icc, hsl],
                                                   start=(icc == 0),
                                                   stop=(icc == IC - 1))
                                wl = [("dW2", w2_done[e])]
                                kk = 2 * cg + half
                                if kk == 0:
                                    wl.append(("eV", gv[(3, 3)]))
                                if kk < 4:
                                    wl.append(("eA", sl_g[(3, kk)]))
                                key = kk - 4
                                if key >= 0 and (e, nt, key) in ye_done:
                                    wl.append(("eV", ye_done[(e, nt, key)]))
                                stage("tensor", f_y, waits=wl, inc="eT")
                                ydone = cur("eT")

                                def f_ye(eng, c=c, half=half, bank=bank,
                                         par=par, hsl=hsl):
                                    eng.tensor_scalar(
                                        stg2[c % 2][:, hsl], bank[:],
                                        gat_s[par][:, c:c + 1], None,
                                        ALU.mult)
                                ywts = [("eT", ydone)]
                                if half == 0 and (e * NB + c - 2) in scat_done:
                                    ywts.append(("dA", scat_done[e * NB + c - 2]))
                                stage("vector", f_ye, waits=ywts, inc="eV")
                                ye_done[(e, nt, 2 * cg + half)] = cur("eV")

                            def f_sc(eng, c=c, par=par):
                                eng.indirect_dma_start(
                                    out[:], IOX2(idxi_s[par][:, c:c + 1], 0),
                                    stg2[c % 2][:], None,
                                    compute_op=ALU.add,
                                ).then_inc(sems["dA"], 16)
                            stage("gpsimd", f_sc,
                                  waits=W("eV", "dA") + [("dG", p1_end["dG"])],
                                  dma=("dA", 16))
                            scat_done[e * NB + c] = cur("dA")

                    w2_free[e] = cur("eT")
                    ye_done[e] = ye_done[(e, NNT - 1, 7)]
                    if e + 1 < E:
                        plan_w2(e + 1)

        def f_fin(eng):
            for n in ("dG", "dX", "dA", "dW2", "dP0", "dP1"):
                if ct[n] > 0:
                    eng.wait_ge(sems[n], ct[n])
        blk.gpsimd(f_fin)

    return nc


def _consts():
    cst = np.zeros((128, 1544), np.float32)
    cst[:, 0:128] = np.eye(128, dtype=np.float32)
    cst[:, 128:256] = np.triu(np.ones((128, 128), np.float32))
    cst[:, 256:384] = 1.0
    bd = np.zeros((128, 2), np.float32)
    bd[0:64, 0] = 1.0
    bd[64:128, 1] = 1.0
    cst[:, 384:386] = bd
    cst[0, 386:514] = 1.0
    cst[:, 514] = 1e-5
    cst[:, 515] = 1e-24
    for e in range(8):
        cst[e, 520 + e * 128: 520 + (e + 1) * 128] = 1.0
    return cst


def kernel(L0_attn_x, L0_attn_Wq, L0_attn_Wk, L0_attn_Wv, L0_attn_Wa,
           L0_attn_ba, L0_attn_Wo, L0_attn_gamma, L0_attn_beta,
           L0_ffn_wg, L0_ffn_w1, L0_ffn_w3, L0_ffn_w2):
    x = np.asarray(L0_attn_x, np.float32)
    B = x.shape[0]

    def fmw(w):
        return np.ascontiguousarray(
            np.asarray(w, np.float32).reshape(8, 128, -1).transpose(1, 0, 2))

    wq_h = fmw(L0_attn_Wq)
    wk_h = fmw(L0_attn_Wk)
    wv_h = fmw(L0_attn_Wv)
    wa_h = fmw(L0_attn_Wa)
    wo_h = np.ascontiguousarray(
        np.asarray(L0_attn_Wo, np.float32).reshape(4, 128, 1024)
        .transpose(1, 0, 2))
    ba_h = np.ascontiguousarray(
        np.asarray(L0_attn_ba, np.float32).reshape(4, 128).T)
    gmb_h = np.ascontiguousarray(np.stack(
        [np.asarray(L0_attn_gamma, np.float32).reshape(8, 128).T,
         np.asarray(L0_attn_beta, np.float32).reshape(8, 128).T], axis=1))
    wgt_h = np.ascontiguousarray(
        np.asarray(L0_ffn_wg, np.float32).T.reshape(8, 128, 8)
        .transpose(1, 0, 2))
    w1t_h = np.ascontiguousarray(
        np.asarray(L0_ffn_w1, np.float32).transpose(0, 2, 1)
        .reshape(8, 8, 128, 2048).transpose(0, 2, 1, 3))
    w3t_h = np.ascontiguousarray(
        np.asarray(L0_ffn_w3, np.float32).transpose(0, 2, 1)
        .reshape(8, 8, 128, 2048).transpose(0, 2, 1, 3))
    w2t_h = np.ascontiguousarray(
        np.asarray(L0_ffn_w2, np.float32).transpose(0, 2, 1)
        .reshape(8, 16, 128, 1024).transpose(0, 2, 1, 3))
    cst_h = _consts()
    Tb = x.shape[1]
    tok_h = np.arange(Tb, dtype=np.float32).reshape(Tb // 128, 128).T.copy()

    nc = build_nc()
    in_maps = []
    for b in range(B):
        in_maps.append({
            "x_fm": np.ascontiguousarray(x[b].T),
            "wq": wq_h, "wk": wk_h, "wv": wv_h, "wa": wa_h, "ba": ba_h,
            "wo": wo_h, "gmb": gmb_h, "wgt": wgt_h,
            "w1t": w1t_h, "w3t": w3t_h, "w2t": w2t_h, "cst": cst_h,
            "tokc": tok_h,
        })
    res = run_bass_kernel_spmd(nc, in_maps, core_ids=list(range(B)))
    outs = np.stack([np.asarray(r["out"])[:Tb] for r in res.results], axis=0)
    return np.ascontiguousarray(outs.astype(np.float32))


# version 30
# speedup vs baseline: 1.0121x; 1.0121x over previous
import os
import numpy as np
import ml_dtypes
import contextlib
import concourse.bass as bass
import concourse.mybir as mybir
from concourse.bass_utils import run_bass_kernel_spmd

F32 = mybir.dt.float32
BF16 = mybir.dt.bfloat16
AF = mybir.ActivationFunctionType
ALU = mybir.AluOpType

T = int(os.environ.get("K_T", "4096"))
D = 1024
HC = 8
DI = 512
NP = 4
NH = 8
DH = 64
E = 8
I_FF = 2048
IC = 16
TT = 512
NTT = T // TT
BFD = T // 128
C = 128
NCAP = max(512, ((3 * T // 8 + 511) // 512) * 512)
NB = NCAP // 128
NCH = TT // C
LN_EPS = 1e-5

PHASE2 = os.environ.get("K_PHASE2", "sparse")


def build_nc():
    nc = bass.Bass("TRN2", target_bir_lowering=False, debug=False)

    x_fm = nc.dram_tensor("x_fm", [D, T], F32, kind="ExternalInput").ap()
    wq = nc.dram_tensor("wq", [128, HC, DI], F32, kind="ExternalInput").ap()
    wk = nc.dram_tensor("wk", [128, HC, DI], F32, kind="ExternalInput").ap()
    wv = nc.dram_tensor("wv", [128, HC, DI], F32, kind="ExternalInput").ap()
    wa = nc.dram_tensor("wa", [128, HC, DI], F32, kind="ExternalInput").ap()
    ba = nc.dram_tensor("ba", [128, NP], F32, kind="ExternalInput").ap()
    wo = nc.dram_tensor("wo", [128, NP, D], F32, kind="ExternalInput").ap()
    gmb = nc.dram_tensor("gmb", [128, 2, HC], F32, kind="ExternalInput").ap()
    wgt = nc.dram_tensor("wgt", [128, HC, E], F32, kind="ExternalInput").ap()
    w1t = nc.dram_tensor("w1t", [E, 128, HC, I_FF], F32, kind="ExternalInput").ap()
    w3t = nc.dram_tensor("w3t", [E, 128, HC, I_FF], F32, kind="ExternalInput").ap()
    w2t = nc.dram_tensor("w2t", [E, 128, IC, D], F32, kind="ExternalInput").ap()
    cst = nc.dram_tensor("cst", [128, 1544], F32, kind="ExternalInput").ap()
    tokc = nc.dram_tensor("tokc", [128, T // 128], F32, kind="ExternalInput").ap()
    out = nc.dram_tensor("out", [T + 1, D], F32, kind="ExternalOutput").ap()
    x2ntm = nc.dram_tensor("x2ntm", [T + 1, D], BF16, kind="Internal").ap()
    NSLOT = 8 * NCAP + 2 * T + 128
    comb_h = nc.dram_tensor("comb_h", [NSLOT, 2], F32, kind="Internal").ap()

    x_r = x_fm.rearrange("(hc p) t -> p hc t", p=128)

    st = contextlib.ExitStack()
    sbuf = lambda n, s, d: st.enter_context(nc.sbuf_tensor(n, s, d))
    psum = lambda n, s, d: st.enter_context(nc.psum_tensor(n, s, d))

    with st:
        wgt_s = sbuf("wgt_s", [128, HC, E], BF16)
        cst_s = sbuf("cst_s", [128, 640], F32)
        cstb_s = sbuf("cstb_s", [128, 130], BF16)
        identb = sbuf("identb", [128, 128], BF16)
        gmb_s = sbuf("gmb_s", [128, 2, HC], F32)
        ba_s = sbuf("ba_s", [128, NP], F32)
        ltms = sbuf("ltms", [128, BFD, E], F32)
        swg = sbuf("swg", [8, 1], F32)

        ident = cst_s[:, 0:128]
        mask_c = cst_s[:, 128:256]
        onesb = cstb_s[:, 0:128]
        bd2 = cstb_s[:, 128:130]
        epsc = cst_s[:, 514:515]
        c24 = cst_s[:, 515:516]

        pall = [psum(f"p{i}", [128, 512], F32) for i in range(8)]
        p0, p1, p2, p3, p4, p5, p6, p7 = pall

        sems = {n: st.enter_context(nc.semaphore(n))
                for n in ("dX", "dL", "dG", "dP0", "dP1", "dR0", "dR1", "dA", "dW2", "dB0", "dB1", "dB2", "dB3", "eT", "eV", "eA", "eG")}
        ct = {n: 0 for n in sems}

        blk = st.enter_context(nc.Block())

        ENG_SEM = {"tensor": "eT", "vector": "eV", "scalar": "eA",
                   "gpsimd": "eG", "sync": None}

        _NOAUTO = {"wait_ge", "wait_op", "nop", "drain", "sem_inc"}

        class _Rec:

            def __init__(self, eng, engname, auto):
                self._e = eng
                self._sem = ENG_SEM[engname]
                self._auto = auto and self._sem is not None
                self.last = None

            def bar(self):
                if self._auto:
                    self._e.wait_ge(sems[self._sem], ct[self._sem])
                elif self.last is not None:
                    sn = self._sem
                    self.last.then_inc(sems[sn], 1)
                    ct[sn] += 1
                    self._e.wait_ge(sems[sn], ct[sn])
                    self.last = None

            def __getattr__(self, name):
                a = getattr(self._e, name)
                if not callable(a):
                    return a

                def call(*args, **kw):
                    r = a(*args, **kw)
                    if hasattr(r, "then_inc") and name not in _NOAUTO:
                        self.last = r
                        if self._auto:
                            r.then_inc(sems[self._sem], 1)
                            ct[self._sem] += 1
                    return r
                return call

        bufs = {}

        def _be(b):
            return bufs.setdefault(b, {"w": None, "r": []})

        def stage(engname, body, reads=(), writes=(), waits=(), inc=None,
                  dma=None):
            wl = [(n, (ct[n] if v is None else v)) for (n, v) in waits]
            for b in reads:
                e = _be(b)
                if e["w"]:
                    wl.append(e["w"])
            for b in writes:
                e = _be(b)
                if e["w"]:
                    wl.append(e["w"])
                wl.extend(e["r"])
            auto = (inc is not None) and engname != "tensor" and dma is None
            if auto:
                own = ENG_SEM[engname]
                wl.append((own, ct[own]))
            if dma is not None:
                wl.append((dma[0], ct[dma[0]]))
            wd = {}
            for n, v in wl:
                if v > 0:
                    wd[n] = max(wd.get(n, 0), v)
            wv = [(sems[n], v) for n, v in wd.items()]

            def f(eng):
                for s, v in wv:
                    eng.wait_ge(s, v)
                rec = _Rec(eng, engname, auto)
                body(rec)
                if inc is not None and not auto:
                    tgt = rec.last if rec.last is not None else eng.nop()
                    tgt.then_inc(sems[inc], 1)
                    ct[inc] += 1
            getattr(blk, engname)(f)
            if dma is not None:
                ct[dma[0]] += dma[1]
                comp = (dma[0], ct[dma[0]])
            elif auto:
                comp = (ENG_SEM[engname], ct[ENG_SEM[engname]])
            elif inc is not None:
                comp = (inc, ct[inc])
            else:
                comp = None
            if comp is not None:
                ws = set(writes)
                for b in reads:
                    if b not in ws:
                        _be(b)["r"].append(comp)
                for b in writes:
                    bufs[b] = {"w": comp, "r": []}
            return dict(ct)

        cur = lambda n: ct[n]
        W = lambda *names: [(n, None) for n in names]

        def ld0(eng):
            eng.dma_start(out=cst_s[:], in_=cst[:, 0:640]).then_inc(sems["dX"], 16)
            eng.dma_start(out=gmb_s[:], in_=gmb[:]).then_inc(sems["dX"], 16)
            eng.dma_start(out=ba_s[:], in_=ba[:]).then_inc(sems["dX"], 16)
        stage("sync", ld0, writes=("cst",), dma=("dX", 48))

        with contextlib.ExitStack() as sA:
            sb1 = lambda n, s, d: sA.enter_context(nc.sbuf_tensor(n, s, d))
            wq_s = sb1("wq_s", [128, HC, DI], BF16)
            wk_s = sb1("wk_s", [128, HC, DI], BF16)
            wv_s = sb1("wv_s", [128, HC, DI], BF16)
            wa_s = sb1("wa_s", [128, HC, DI], BF16)
            wo_s = sb1("wo_s", [128, NP, D], BF16)
            nba = sb1("nba", [128, NP], F32)
            xb2 = [sb1(f"xbuf{j}", [128, HC, TT], BF16) for j in range(2)]
            t0b2 = [sb1(f"t0buf{j}", [128, HC, TT], BF16) for j in range(2)]
            sqc = sb1("sqc", [128, HC, TT], BF16)
            xn = sb1("xn", [128, HC, TT], BF16)
            sc0 = sb1("sc0", [128, NP, TT], F32)
            mu1 = sb1("mu1", [128, TT], F32)
            r1 = sb1("r1", [128, TT], F32)
            r2f = sb1("r2f", [128, TT], F32)
            mu3 = sb1("mu3", [128, TT], F32)
            r3 = sb1("r3", [128, TT], F32)
            q_s = sb1("q_s", [128, NP, TT], BF16)
            k_s = sb1("k_s", [128, NP, TT], BF16)
            al_s = sb1("al_s", [128, NP, TT], F32)
            v2 = [sb1(f"vbuf{j}", [128, NCH, DI], BF16) for j in range(2)]
            o_s = sb1("o_s", [128, NP, TT], BF16)
            e1_s = sb1("e1_s", [128, NP, TT], BF16)
            e2_s = sb1("e2_s", [128, NP, TT], BF16)
            e1l2 = [sb1(f"e1l{j}", [128, NP, NCH], F32) for j in range(2)]
            qt2 = [sb1(f"qt{j}", [128, NP, TT], BF16) for j in range(2)]
            ktb2 = [sb1(f"ktb{j}", [128, NP, TT], BF16) for j in range(2)]
            khb2 = [sb1(f"khb{j}", [128, NP, TT], BF16) for j in range(2)]
            rlnp = sb1("rlnp", [2, NP, TT], F32)
            rnt2 = [sb1(f"rnt{j}", [128, NCH, NP * 2], F32) for j in range(2)]
            pt_s = sb1("pt_s", [128, NH, C], BF16)
            khtm = sb1("khtm", [128, NH, DH], BF16)
            s32 = sb1("s32", [128, NP, DH], F32)
            sbf = sb1("sbf", [128, NP, DH], BF16)
            xtm = sb1("xtm", [128, D], F32)
            xntm = sb1("xntm", [128, D], BF16)
            mre = sb1("mre", [128, 2], F32)
            lgs = sb1("lgs", [8, TT], F32)

            def ldw(eng):
                for dst, src in ((wq_s, wq), (wk_s, wk), (wv_s, wv),
                                 (wa_s, wa), (wo_s, wo), (wgt_s, wgt)):
                    eng.dma_start(out=dst[:], in_=src[:]).then_inc(sems["dG"], 16)
                eng.dma_start(out=cstb_s[:], in_=cst[:, 256:386]).then_inc(
                    sems["dG"], 16)
                eng.dma_start(out=identb[:], in_=cst[:, 0:128]).then_inc(
                    sems["dG"], 16)
            stage("gpsimd", ldw, reads=("cst",), writes=("wts",),
                  dma=("dG", 128))
            wload_done = cur("dG")

            def init(eng):
                eng.memset(s32[:], 0.0)
                eng.memset(sbf[:], 0.0)
                eng.memset(xntm[:], 0.0)
                eng.tensor_scalar(nba[:], ba_s[:], -1.0, None, ALU.mult)
            stage("vector", init, reads=("cst",),
                  writes=("s", "nba", "xntm"), inc="eV")

            def f_trash(eng):
                eng.dma_start(out=x2ntm[T:T + 1, :],
                              in_=xntm[0:1, :]).then_inc(sems["dG"], 16)
            stage("gpsimd", f_trash, reads=("xntm",), dma=("dG", 16))

            def f_swg(eng):
                for hc in range(HC):
                    eng.matmul(p2[0:8, 0:1], wgt_s[:, hc, :], onesb[:, 0:1],
                               start=(hc == 0), stop=(hc == HC - 1))
            stage("tensor", f_swg, reads=("wts",), writes=("p2",), inc="eT")
            stage("vector", lambda eng: eng.tensor_copy(swg[:], p2[0:8, 0:1]),
                  reads=("p2",), writes=("swg",), inc="eV")


            def plan_xl(i):
                par = i % 2
                tsl = slice(i * TT, (i + 1) * TT)

                def f_xl(eng, par=par, tsl=tsl):
                    eng.dma_start(out=xb2[par][:],
                                  in_=x_r[:, :, tsl]).then_inc(sems["dL"], 16)
                stage("gpsimd", f_xl, writes=(f"xb{par}",), dma=("dL", 16))

            def plan_F(i, part):
                par = i % 2
                xb = xb2[par]
                t0b = t0b2[par]
                XB = f"xb{par}"
                T0 = f"t0b{par}"
                if part == 0:
                    stage("scalar",
                          lambda eng: eng.activation(sqc[:], xb[:], AF.Square),
                          reads=(XB,), writes=("sqc",), inc="eA")

                    def f_st1(eng):
                        for hc in range(HC):
                            eng.matmul(p0[:], onesb, xb[:, hc, :],
                                       start=(hc == 0), stop=(hc == HC - 1))
                        for hc in range(HC):
                            eng.matmul(p1[:], onesb, sqc[:, hc, :],
                                       start=(hc == 0), stop=(hc == HC - 1))
                    stage("tensor", f_st1, reads=(XB, "sqc", "wts"),
                          writes=("p0", "p1"), inc="eT")

                    def f_mv1(eng):
                        eng.tensor_scalar(mu1[:], p0[:], 1.0 / D, None, ALU.mult)
                        eng.tensor_scalar(r1[:], p1[:], 1.0 / D, None, ALU.mult)
                        eng.bar()
                        eng.tensor_tensor(sc0[:, 0, :], mu1[:], mu1[:], ALU.mult)
                        eng.bar()
                        eng.tensor_tensor(r1[:], r1[:], sc0[:, 0, :],
                                          ALU.subtract)
                    stage("vector", f_mv1, reads=("p0", "p1"),
                          writes=("mu1", "r1", "sc0"), inc="eV")

                    def f_rst1(eng):
                        eng.activation(r1[:], r1[:], AF.Ln, bias=epsc)
                        eng.bar()
                        eng.activation(r1[:], r1[:], AF.Exp, scale=-0.5)
                    stage("scalar", f_rst1, reads=("cst",),
                          writes=("r1",), inc="eA")

                    def f_ap1(eng):
                        for g in range(2):
                            for h4 in range(NP):
                                hc = g * NP + h4
                                eng.tensor_tensor(sc0[:, h4, :], xb[:, hc, :],
                                                  mu1[:], ALU.subtract)
                            eng.bar()
                            for h4 in range(NP):
                                hc = g * NP + h4
                                eng.tensor_tensor(t0b[:, hc, :], sc0[:, h4, :],
                                                  r1[:], ALU.mult)
                            eng.bar()
                    stage("vector", f_ap1, reads=(XB, "mu1", "r1"),
                          writes=("sc0", T0), inc="eV")
                elif part == 1:
                    def f_r2v(eng):
                        eng.tensor_tensor(sc0[:, 2, :], r1[:], r1[:], ALU.mult)
                        eng.bar()
                        eng.tensor_scalar(sc0[:, 2, :], sc0[:, 2, :],
                                          -LN_EPS, 1.0, ALU.mult, ALU.add)
                    stage("vector", f_r2v, reads=("r1",), writes=("sc0",),
                          inc="eV")

                    def f_rst2(eng):
                        eng.activation(r2f[:], sc0[:, 2, :], AF.Ln, bias=epsc)
                        eng.bar()
                        eng.activation(r2f[:], r2f[:], AF.Exp, scale=-0.5)
                    stage("scalar", f_rst2, reads=("cst", "sc0"),
                          writes=("r2f",), inc="eA")

                    def f_ap2(eng):
                        for g in range(2):
                            for h4 in range(NP):
                                hc = g * NP + h4
                                eng.tensor_tensor(sc0[:, h4, :], t0b[:, hc, :],
                                                  r2f[:], ALU.mult)
                            eng.bar()
                            for h4 in range(NP):
                                hc = g * NP + h4
                                eng.tensor_scalar(xn[:, hc, :], sc0[:, h4, :],
                                                  gmb_s[:, 0, hc:hc + 1],
                                                  gmb_s[:, 1, hc:hc + 1],
                                                  ALU.mult, ALU.add)
                            eng.bar()
                    stage("vector", f_ap2, reads=(T0, "r2f", "cst"),
                          writes=("sc0", "xn"), inc="eV")
                elif part == 2:
                    for wsb, dname, kind in ((wq_s, "q", "v"), (wk_s, "k", "v"),
                                             (wa_s, "al", "a")):
                        dst = {"q": q_s, "k": k_s, "al": al_s}[dname]
                        for icp in range(NP):
                            bank = pall[icp % 2]
                            bn = f"p{icp % 2}"

                            def f_t(eng, wsb=wsb, icp=icp, bank=bank):
                                for hc in range(HC):
                                    eng.matmul(bank[:],
                                               wsb[:, hc, icp * 128:(icp + 1) * 128],
                                               xn[:, hc, :],
                                               start=(hc == 0), stop=(hc == HC - 1))
                            stage("tensor", f_t, reads=("xn", "wts"),
                                  writes=(bn,), inc="eT")
                            if kind == "v":
                                stage("vector",
                                      lambda eng, dst=dst, icp=icp, bank=bank:
                                      eng.tensor_copy(dst[:, icp, :], bank[:]),
                                      reads=(bn,), writes=(dname,), inc="eV")
                            else:
                                def f_sp(eng, icp=icp, bank=bank):
                                    eng.activation(sc0[:, icp, :], bank[:],
                                                   AF.Exp, scale=-1.0,
                                                   bias=nba[:, icp:icp + 1])
                                    eng.bar()
                                    eng.activation(al_s[:, icp, :],
                                                   sc0[:, icp, :],
                                                   AF.Ln, bias=1.0)
                                stage("scalar", f_sp, reads=(bn, "nba"),
                                      writes=("sc0", "al"), inc="eA")
                    for tc in range(NCH):
                        bank = pall[tc % 2]
                        bn = f"p{tc % 2}"

                        def f_tv(eng, tc=tc, bank=bank):
                            for hc in range(HC):
                                eng.matmul(bank[:], xn[:, hc, tc * C:(tc + 1) * C],
                                           wv_s[:, hc, :],
                                           start=(hc == 0), stop=(hc == HC - 1))
                        stage("tensor", f_tv, reads=("xn", "wts"),
                              writes=(bn,), inc="eT")
                        stage("vector",
                              lambda eng, tc=tc, bank=bank, par=par:
                              eng.tensor_copy(v2[par][:, tc, :], bank[:]),
                              reads=(bn,), writes=(f"v{par}",), inc="eV")
                elif part == 3:
                    def g_scan(eng):
                        for pc in range(NP):
                            for lc in range(NCH):
                                csl = slice(lc * C, (lc + 1) * C)
                                eng.tensor_tensor_scan(sc0[:, pc, csl],
                                                       al_s[:, pc, csl],
                                                       al_s[:, pc, csl], 0.0,
                                                       ALU.add, ALU.bypass)
                    stage("vector", g_scan, reads=("al",), writes=("sc0",),
                          inc="eV")

                    def g_exp(eng, par=par):
                        eng.activation(e1_s[:], sc0[:], AF.Exp, scale=-1.0)
                        eng.activation(e2_s[:], sc0[:], AF.Exp)
                        eng.activation(e1l2[par][:], sc0[:, :, C - 1::C],
                                       AF.Exp, scale=-1.0)
                    stage("scalar", g_exp, reads=("sc0",),
                          writes=("e1", "e2", f"e1l{par}"), inc="eA")

                    stage("scalar",
                          lambda eng: eng.activation(sqc[:, 0:NP, :], k_s[:],
                                                     AF.Square),
                          reads=("k",), writes=("sqc",), inc="eA")

                    def g_ss(eng):
                        for pc in range(NP):
                            eng.matmul(p0[32 * pc:32 * pc + 2, 0:TT], bd2,
                                       sqc[:, pc, :], start=True, stop=True,
                                       tile_position=(0, 32 * pc))
                    stage("tensor", g_ss, reads=("sqc", "wts"),
                          writes=("p0",), inc="eT")

                    def g_rln(eng):
                        for pc in range(NP):
                            eng.tensor_copy(rlnp[:, pc, :],
                                            p0[32 * pc:32 * pc + 2, 0:TT])
                    stage("vector", g_rln, reads=("p0",), writes=("rlnp",),
                          inc="eV")

                    def g_rle(eng):
                        eng.activation(rlnp[:], rlnp[:], AF.Ln, bias=c24[0:2, :])
                        eng.bar()
                        eng.activation(rlnp[:], rlnp[:], AF.Exp, scale=-0.5)
                    stage("scalar", g_rle, reads=("cst",), writes=("rlnp",),
                          inc="eA")
                elif part == 4:
                    def g_rlt(eng):
                        for pc in range(NP):
                            for lc in range(NCH):
                                col = (pc * NCH + lc) * 2
                                eng.transpose(p1[:, col:col + 2],
                                              rlnp[:, pc, lc * C:(lc + 1) * C],
                                              ident[0:2, 0:2])
                    stage("tensor", g_rlt, reads=("rlnp", "cst"),
                          writes=("p1",), inc="eT")

                    def g_rnte(eng, par=par):
                        for pc in range(NP):
                            eng.tensor_copy(
                                rnt2[par][:, :, 2 * pc:2 * pc + 2],
                                p1[:, pc * 2 * NCH:(pc + 1) * 2 * NCH]
                                .rearrange("p (t j) -> p t j", j=2))
                    stage("vector", g_rnte, reads=("p1",),
                          writes=(f"rnt{par}",), inc="eV")

                    def g_mul(eng, par=par):
                        eng.tensor_tensor(qt2[par][:], q_s[:], e1_s[:], ALU.mult)
                        eng.tensor_tensor(ktb2[par][:], k_s[:], e2_s[:], ALU.mult)
                        eng.bar()
                        for pc in range(NP):
                            for lc in range(NCH):
                                csl = slice(lc * C, (lc + 1) * C)
                                eng.tensor_scalar(khb2[par][:, pc, csl],
                                                  ktb2[par][:, pc, csl],
                                                  e1l2[par][:, pc, lc:lc + 1],
                                                  None, ALU.mult)
                    stage("vector", g_mul,
                          reads=("q", "k", "e1", "e2", f"e1l{par}"),
                          writes=(f"qt{par}", f"ktb{par}", f"khb{par}"),
                          inc="eV")

            def plan_B(i, part):
                par = i % 2
                xb = xb2[par]
                t0b = t0b2[par]
                qt = qt2[par]
                kt = ktb2[par]
                kh = khb2[par]
                e1l = e1l2[par]
                rnt = rnt2[par]
                v_s = v2[par]
                XB = f"xb{par}"
                T0 = f"t0b{par}"
                QT = f"qt{par}"
                KT = f"ktb{par}"
                KH = f"khb{par}"
                EL = f"e1l{par}"
                RN = f"rnt{par}"
                VS = f"v{par}"
                if part in (0, 1):
                    for lc in ((0, 1) if part == 0 else (2, 3)):
                        csl = slice(lc * C, (lc + 1) * C)

                        def g_t2(eng, csl=csl, qt=qt, kt=kt, kh=kh):
                            for pc in range(NP):
                                for j in range(2):
                                    bank = p2 if j == 0 else p3
                                    eng.matmul(bank[:, pc * C:(pc + 1) * C],
                                               kt[j * 64:(j + 1) * 64, pc, csl],
                                               qt[j * 64:(j + 1) * 64, pc, csl],
                                               start=True, stop=True)
                            for pc in range(NP):
                                eng.matmul(p6[:, pc * C:(pc + 1) * C],
                                           kh[:, pc, csl], identb[:],
                                           start=True, stop=True)
                        stage("tensor", g_t2, reads=(QT, KT, KH, "wts"),
                              writes=("p2", "p3", "p6"), inc="eT")

                        def g_v4(eng, lc=lc, rnt=rnt):
                            for pc in range(NP):
                                for j in range(2):
                                    h = 2 * pc + j
                                    bank = p2 if j == 0 else p3
                                    eng.scalar_tensor_tensor(
                                        pt_s[:, h, :],
                                        bank[:, pc * C:(pc + 1) * C],
                                        rnt[:, lc, pc * 2 + j:pc * 2 + j + 1],
                                        mask_c, ALU.mult, ALU.mult)
                            for pc in range(NP):
                                for j in range(2):
                                    h = 2 * pc + j
                                    eng.tensor_scalar(
                                        khtm[:, h, :],
                                        p6[:, pc * C + j * 64:pc * C + (j + 1) * 64],
                                        rnt[:, lc, pc * 2 + j:pc * 2 + j + 1],
                                        None, ALU.mult)
                        stage("vector", g_v4,
                              reads=("p2", "p3", "p6", RN, "cst"),
                              writes=("pt", "khtm"), inc="eV")

                        def g_t3(eng, lc=lc, csl=csl, qt=qt, v_s=v_s):
                            for pc in range(NP):
                                for j in range(2):
                                    h = 2 * pc + j
                                    eng.matmul(
                                        p4[j * 64:(j + 1) * 64, pc * C:(pc + 1) * C],
                                        v_s[:, lc, h * 64:(h + 1) * 64],
                                        pt_s[:, h, :],
                                        start=True, stop=False,
                                        tile_position=(0, 64 * j))
                                    eng.matmul(
                                        p4[j * 64:(j + 1) * 64, pc * C:(pc + 1) * C],
                                        sbf[j * 64:(j + 1) * 64, pc, :],
                                        qt[j * 64:(j + 1) * 64, pc, csl],
                                        start=False, stop=True,
                                        tile_position=(64 * j, 64 * j))
                            for pc in range(NP):
                                for j in range(2):
                                    h = 2 * pc + j
                                    eng.matmul(
                                        p5[j * 64:(j + 1) * 64, pc * 64:(pc + 1) * 64],
                                        khtm[:, h, :],
                                        v_s[:, lc, h * 64:(h + 1) * 64],
                                        start=True, stop=True,
                                        tile_position=(0, 64 * j))
                        stage("tensor", g_t3, reads=(VS, "pt", "khtm", "s", QT),
                              writes=("p4", "p5"), inc="eT")

                        def g_v5(eng, lc=lc, e1l=e1l):
                            for pc in range(NP):
                                eng.tensor_copy(o_s[:, pc, lc * C:(lc + 1) * C],
                                                p4[:, pc * C:(pc + 1) * C])
                            for pc in range(NP):
                                eng.tensor_scalar(s32[:, pc, :], s32[:, pc, :],
                                                  e1l[:, pc, lc:lc + 1], None,
                                                  ALU.mult)
                            eng.bar()
                            eng.tensor_tensor(
                                s32.ap().rearrange("p a b -> p (a b)"),
                                s32.ap().rearrange("p a b -> p (a b)"),
                                p5[:, 0:NP * DH], ALU.add)
                            eng.bar()
                            eng.tensor_copy(sbf[:], s32[:])
                        stage("vector", g_v5, reads=("p4", "p5", EL),
                              writes=("o", "s"), inc="eV")
                elif part == 2:
                    def f_x2p(eng, t0b=t0b, xb=xb):
                        eng.tensor_tensor(
                            t0b.ap().rearrange("p a b -> p (a b)"),
                            t0b.ap().rearrange("p a b -> p (a b)"),
                            xb.ap().rearrange("p a b -> p (a b)"), ALU.add)
                    stage("vector", f_x2p, reads=(XB,), writes=(T0,), inc="eV")
                    for hc in range(HC):
                        bank = pall[(2, 3, 6, 7)[hc % 4]]
                        bn = f"p{(2, 3, 6, 7)[hc % 4]}"

                        def f_op(eng, hc=hc, bank=bank):
                            for icp in range(NP):
                                eng.matmul(bank[:],
                                           wo_s[:, icp, hc * 128:(hc + 1) * 128],
                                           o_s[:, icp, :],
                                           start=(icp == 0), stop=(icp == NP - 1))
                        stage("tensor", f_op, reads=("o", "wts"),
                              writes=(bn,), inc="eT")

                        def f_x2a(eng, hc=hc, bank=bank, xb=xb, t0b=t0b):
                            eng.tensor_tensor(xb[:, hc, :], bank[:],
                                              t0b[:, hc, :], ALU.add)
                        stage("vector", f_x2a, reads=(bn, T0), writes=(XB,),
                              inc="eV")
                elif part == 3:
                    stage("scalar",
                          lambda eng, xb=xb: eng.activation(sqc[:], xb[:],
                                                            AF.Square),
                          reads=(XB,), writes=("sqc",), inc="eA")

                    def f_st3(eng, xb=xb):
                        for hc in range(HC):
                            eng.matmul(p6[:], onesb, xb[:, hc, :],
                                       start=(hc == 0), stop=(hc == HC - 1))
                        for hc in range(HC):
                            eng.matmul(p7[:], onesb, sqc[:, hc, :],
                                       start=(hc == 0), stop=(hc == HC - 1))
                    stage("tensor", f_st3, reads=(XB, "sqc", "wts"),
                          writes=("p6", "p7"), inc="eT")

                    def f_mv3(eng):
                        eng.tensor_scalar(mu3[:], p6[:], 1.0 / D, None, ALU.mult)
                        eng.tensor_scalar(r3[:], p7[:], 1.0 / D, None, ALU.mult)
                        eng.bar()
                        eng.tensor_tensor(xtm[:, 0:TT], mu3[:], mu3[:], ALU.mult)
                        eng.bar()
                        eng.tensor_tensor(r3[:], r3[:], xtm[:, 0:TT], ALU.subtract)
                    stage("vector", f_mv3, reads=("p6", "p7"),
                          writes=("mu3", "r3", "xtm"), inc="eV")

                    def f_rst3(eng):
                        eng.activation(r3[:], r3[:], AF.Ln, bias=epsc)
                        eng.bar()
                        eng.activation(r3[:], r3[:], AF.Exp, scale=-0.5)
                    stage("scalar", f_rst3, reads=("cst",), writes=("r3",),
                          inc="eA")

                    for tc in range(NCH):
                        def f_tr(eng, tc=tc, xb=xb):
                            for hc in range(HC):
                                bank = p2 if hc < 4 else p3
                                eng.matmul(
                                    bank[:, (hc % 4) * 128:(hc % 4 + 1) * 128],
                                    xb[:, hc, tc * C:(tc + 1) * C], identb[:],
                                    start=True, stop=True)
                            eng.transpose(p4[:, 0:1],
                                          mu3[0:1, tc * C:(tc + 1) * C],
                                          ident[0:1, 0:1])
                            eng.transpose(p4[:, 1:2],
                                          r3[0:1, tc * C:(tc + 1) * C],
                                          ident[0:1, 0:1])
                        stage("tensor", f_tr,
                              reads=(XB, "mu3", "r3", "wts", "cst"),
                              writes=("p2", "p3", "p4"), inc="eT")

                        def f_tre(eng, tc=tc):
                            eng.tensor_copy(xtm[:, 0:512], p2[:, 0:512])
                            eng.tensor_copy(xtm[:, 512:1024], p3[:, 0:512])
                            eng.tensor_copy(mre[:], p4[:, 0:2])
                            eng.bar()
                            eng.tensor_scalar(xntm[:], xtm[:], mre[:, 0:1],
                                              mre[:, 1:2], ALU.subtract, ALU.mult)
                        stage("vector", f_tre, reads=("p2", "p3", "p4"),
                              writes=("xtm", "xntm", "mre"), inc="eV")

                        def f_tro(eng, tc=tc, i=i):
                            r0 = i * TT + tc * C
                            eng.dma_start(out=out[r0:r0 + 128, :],
                                          in_=xtm[:]).then_inc(sems["dG"], 16)
                            eng.dma_start(out=x2ntm[r0:r0 + 128, :],
                                          in_=xntm[:]).then_inc(sems["dG"], 16)
                        stage("gpsimd", f_tro, reads=("xtm", "xntm"),
                              dma=("dG", 32))
                elif part == 4:
                    def f_lg(eng, xb=xb):
                        for hc in range(HC):
                            eng.matmul(p5[0:8, 0:TT], wgt_s[:, hc, :],
                                       xb[:, hc, :],
                                       start=(hc == 0), stop=(hc == HC - 1))
                    stage("tensor", f_lg, reads=(XB, "wts"), writes=("p5",),
                          inc="eT")

                    def f_lge(eng):
                        eng.tensor_scalar(lgs[:], mu3[0:8, :], swg[0:8, 0:1],
                                          None, ALU.mult)
                        eng.bar()
                        eng.tensor_tensor(lgs[:], p5[0:8, 0:TT], lgs[:],
                                          ALU.subtract)
                        eng.bar()
                        eng.tensor_tensor(lgs[:], lgs[:], r3[0:8, :], ALU.mult)
                    stage("vector", f_lge, reads=("p5", "mu3", "r3", "swg"),
                          writes=("lgs",), inc="eV")

                    def f_lgt(eng):
                        for tc in range(NCH):
                            eng.transpose(p4[:, 8 * tc:8 * tc + 8],
                                          lgs[0:8, tc * C:(tc + 1) * C],
                                          ident[0:8, 0:8])
                    stage("tensor", f_lgt, reads=("lgs", "cst"),
                          writes=("p4",), inc="eT")
                    stage("vector",
                          lambda eng, i=i:
                          eng.tensor_copy(ltms[:, i * NCH:(i + 1) * NCH, :],
                                          p4[:, 0:8 * NCH]
                                          .rearrange("p (a b) -> p a b", b=E)),
                          reads=("p4",), writes=("ltms",), inc="eV")

            for i in range(NTT + 1):
                fe = i < NTT
                be = i >= 1
                if fe:
                    plan_xl(i)
                if be:
                    plan_B(i - 1, 0)
                if fe:
                    plan_F(i, 0)
                if be:
                    plan_B(i - 1, 1)
                if fe:
                    plan_F(i, 1)
                if be:
                    plan_B(i - 1, 2)
                if fe:
                    plan_F(i, 2)
                if be:
                    plan_B(i - 1, 3)
                if fe:
                    plan_F(i, 3)
                    plan_F(i, 4)
                if be:
                    plan_B(i - 1, 4)

        with contextlib.ExitStack() as sR:
            sbr = lambda n, s, d: sR.enter_context(nc.sbuf_tensor(n, s, d))
            mfm = sbr("mfm", [8, T], F32)
            maxv = sbr("maxv", [128, BFD, 8], F32)
            argt = sbr("argt", [128, BFD, 8], mybir.dt.uint32)
            d12 = sbr("d12", [128, BFD], F32)
            w1v = sbr("w1v", [128, BFD], F32)
            w2v = sbr("w2v", [128, BFD], F32)
            wtm = sbr("wtm", [128, BFD, 8], F32)
            selA = sbr("selA", [128, BFD, 8], F32)
            selB = sbr("selB", [128, BFD, 8], F32)
            tok_s = sbr("tok_s", [128, BFD], F32)
            pfx = sbr("pfx", [8, T], F32)
            ptm = sbr("ptm", [128, BFD, 8], F32)
            sl1 = sbr("sl1", [128, BFD], F32)
            sl2 = sbr("sl2", [128, BFD], F32)
            sli1 = sbr("sli1", [128, BFD], mybir.dt.int32)
            sli2 = sbr("sli2", [128, BFD], mybir.dt.int32)
            qr1 = sbr("qr1", [128, BFD], F32)
            qr2 = sbr("qr2", [128, BFD], F32)
            fill_s = sbr("fill_s", [128, 2 * ((NSLOT + 127) // 128)], F32)
            tg1 = sbr("tg1", [128, BFD, 2], F32)
            tg2 = sbr("tg2", [128, BFD, 2], F32)

            def f_tok(eng):
                eng.dma_start(out=tok_s[:], in_=tokc[:]).then_inc(
                    sems["dX"], 16)
            stage("sync", f_tok,
                  waits=W("eV", "eA", "eT", "eG", "dG"), dma=("dX", 16))
            tok_done = cur("dX")

            stage("vector", lambda eng: eng.memset(d12[0:1, 0:1], 0.0),
                  waits=W("eA", "eT", "eG") + [("dX", tok_done)], inc="eV")
            stage("scalar",
                  lambda eng: eng.copy(d12[0:1, 1:2], d12[0:1, 0:1]),
                  waits=W("eV", "eT", "eG"), inc="eA")
            stage("gpsimd", lambda eng: eng.memset(d12[0:1, 2:3], 0.0),
                  waits=W("eV", "eA", "eT"), inc="eG")

            def f_rmax(eng):
                for bi in range(BFD):
                    eng.max(out=maxv[:, bi, :], in_=ltms[:, bi, :])
                eng.bar()
                for bi in range(BFD):
                    eng.max_index(out=argt[:, bi, :], in_max=maxv[:, bi, :],
                                  in_values=ltms[:, bi, :])
                eng.tensor_tensor(d12[:], maxv[:, :, 0], maxv[:, :, 1],
                                  ALU.subtract)
            stage("vector", f_rmax, waits=W("eA", "eG"), inc="eV")

            def f_rsig(eng):
                eng.activation(w1v[:], d12[:], AF.Sigmoid)
                eng.activation(w2v[:], d12[:], AF.Sigmoid, scale=-1.0)
            stage("scalar", f_rsig, waits=W("eV"), inc="eA")

            def f_rw(eng):
                for e in range(E):
                    eng.tensor_scalar(selA[:, :, e], argt[:, :, 0], float(e),
                                      None, ALU.is_equal)
                    eng.tensor_scalar(selB[:, :, e], argt[:, :, 1], float(e),
                                      None, ALU.is_equal)
                eng.bar()
                eng.tensor_tensor(wtm.ap().rearrange("p a b -> p (a b)"),
                                  selA.ap().rearrange("p a b -> p (a b)"),
                                  selB.ap().rearrange("p a b -> p (a b)"),
                                  ALU.add)
            stage("vector", f_rw, waits=W("eA"), inc="eV")

            for bi in range(BFD):
                bank = pall[2 + (bi % 2)]
                stage("tensor",
                      lambda eng, bi=bi, bank=bank:
                      eng.transpose(bank[0:8, 0:128], wtm[:, bi, :], ident),
                      waits=W("eV"), inc="eT")
                stage("vector",
                      lambda eng, bi=bi, bank=bank:
                      eng.tensor_copy(mfm[:, bi::BFD], bank[0:8, 0:128]),
                      waits=W("eT"), inc="eV")

            def f_pfx(eng):
                eng.tensor_tensor_scan(pfx[:], mfm[:], mfm[:], 0.0,
                                       ALU.add, ALU.bypass)
            stage("vector", f_pfx, waits=W("eV"), inc="eV")

            for bi in range(BFD):
                bank = pall[2 + (bi % 2)]
                stage("tensor",
                      lambda eng, bi=bi, bank=bank:
                      eng.transpose(bank[:, 0:8], pfx[0:8, bi::BFD],
                                    ident[0:8, 0:8]),
                      waits=W("eV"), inc="eT")
                stage("vector",
                      lambda eng, bi=bi, bank=bank:
                      eng.tensor_copy(ptm[:, bi, :], bank[:, 0:8]),
                      waits=W("eT"), inc="eV")

            TRASH = 8 * NCAP
            AX = mybir.AxisListType.X

            def f_slot(eng):
                for e in range(E):
                    eng.scalar_tensor_tensor(wtm[:, :, e], ptm[:, :, e],
                                             float(e * NCAP - 1),
                                             selA[:, :, e], ALU.add, ALU.mult)
                eng.bar()
                eng.tensor_reduce(sl1[:], wtm.ap(), AX, ALU.add)
                eng.bar()
                for e in range(E):
                    eng.scalar_tensor_tensor(wtm[:, :, e], ptm[:, :, e],
                                             float(e * NCAP - 1),
                                             selB[:, :, e], ALU.add, ALU.mult)
                eng.bar()
                eng.tensor_reduce(sl2[:], wtm.ap(), AX, ALU.add)
                eng.bar()
                for e in range(E):
                    eng.scalar_tensor_tensor(wtm[:, :, e], ptm[:, :, e],
                                             -1.0, selA[:, :, e],
                                             ALU.add, ALU.mult)
                eng.bar()
                eng.tensor_reduce(qr1[:], wtm.ap(), AX, ALU.add)
                eng.bar()
                for e in range(E):
                    eng.scalar_tensor_tensor(wtm[:, :, e], ptm[:, :, e],
                                             -1.0, selB[:, :, e],
                                             ALU.add, ALU.mult)
                eng.bar()
                eng.tensor_reduce(qr2[:], wtm.ap(), AX, ALU.add)
                eng.bar()
                eng.tensor_scalar(d12[:], qr1[:], float(NCAP), None, ALU.is_ge)
                eng.bar()
                eng.scalar_tensor_tensor(sl1[:], d12[:], float(TRASH),
                                         sl1[:], ALU.mult, ALU.max)
                eng.bar()
                eng.tensor_scalar(d12[:], qr2[:], float(NCAP), None, ALU.is_ge)
                eng.bar()
                eng.scalar_tensor_tensor(sl2[:], d12[:], float(TRASH),
                                         sl2[:], ALU.mult, ALU.max)
                eng.bar()
                eng.tensor_copy(sli1[:], sl1[:])
                eng.tensor_copy(sli2[:], sl2[:])
                eng.memset(fill_s[:, 0::2], float(T))
                eng.memset(fill_s[:, 1::2], 0.0)
                eng.tensor_copy(tg1[:, :, 0], tok_s[:])
                eng.tensor_copy(tg2[:, :, 0], tok_s[:])
                eng.tensor_copy(tg1[:, :, 1], w1v[:])
                eng.tensor_copy(tg2[:, :, 1], w2v[:])
            stage("vector", f_slot, waits=W("eV"), inc="eV")

            def f_fill(eng):
                eng.dma_start(out=comb_h.rearrange("(p c) v -> p (c v)",
                                                   p=128),
                              in_=fill_s[:]).then_inc(sems["dG"], 16)
            stage("gpsimd", f_fill, waits=W("eV"), dma=("dG", 16))

            from concourse.bass_core_rust import IndirectOffsetOnAxis as IOX
            def f_scat(eng):
                for bi in range(BFD):
                    eng.indirect_dma_start(
                        comb_h[:], IOX(sli1[:, bi:bi + 1], 0),
                        tg1[:, bi, :], None).then_inc(sems["dG"], 16)
                    eng.indirect_dma_start(
                        comb_h[:], IOX(sli2[:, bi:bi + 1], 0),
                        tg2[:, bi, :], None).then_inc(sems["dG"], 16)
            stage("gpsimd", f_scat, waits=W("dG", "eV"),
                  dma=("dG", BFD * 2 * 16))

            p1_end = dict(ct)


        if PHASE2 != "none":
            from concourse.bass_core_rust import IndirectOffsetOnAxis as IOX2
            NNT = NCAP // 512
            with contextlib.ExitStack() as sB:
                sb2 = lambda n, s, d: sB.enter_context(nc.sbuf_tensor(n, s, d))
                w13 = sb2("w13", [128, 2, HC, I_FF], BF16)
                w2s = sb2("w2s", [128, IC, D], BF16)
                xg = [sb2("xg0", [128, HC, NCAP], BF16),
                      sb2("xg1", [128, HC, NCAP], BF16)]
                g_s = sb2("g_s", [128, IC, 512], BF16)
                silu_s = sb2("silu_s", [128, 4, 512], BF16)
                gtm = [sb2(f"gtm{j}", [128, D], BF16) for j in range(4)]
                stg2 = [sb2("stg20", [128, D], F32), sb2("stg21", [128, D], F32)]
                comb_s = [sb2("comb0", [128, NB, 2], F32),
                          sb2("comb1", [128, NB, 2], F32)]
                gat_s = [sb2("gat0", [128, NB], F32), sb2("gat1", [128, NB], F32)]
                idxi_s = [sb2("idxi0", [128, NB], mybir.dt.int32),
                          sb2("idxi1", [128, NB], mybir.dt.int32)]

                w13_done = {}
                w2_done = {}
                w13_free = {}
                w2_free = {}
                rb_done = {}
                cast_done = {}
                gath_done = {}
                tr_done = {}
                h_done = {}
                evac_done = {}
                scat_done = {}
                ye_done = {}
                gv = {}

                def plan_w13(e, ic4, wait_eT):
                    bsn = "dB%d" % ic4

                    def f_w13(eng, e=e, ic4=ic4, bsn=bsn):
                        eng.dma_start(
                            out=w13[:, 0, :, ic4 * 512:(ic4 + 1) * 512],
                            in_=w1t[e, :, :, ic4 * 512:(ic4 + 1) * 512]
                        ).then_inc(sems[bsn], 16)
                        eng.dma_start(
                            out=w13[:, 1, :, ic4 * 512:(ic4 + 1) * 512],
                            in_=w3t[e, :, :, ic4 * 512:(ic4 + 1) * 512]
                        ).then_inc(sems[bsn], 16)
                    wts = [("eT", wait_eT)] if wait_eT else                         [("eV", p1_end["eV"]), ("eA", p1_end["eA"]),
                         ("eT", p1_end["eT"]), ("dG", p1_end["dG"])]
                    stage("gpsimd", f_w13, waits=wts, dma=(bsn, 32))
                    w13_done[(e, ic4)] = (bsn, cur(bsn))

                def plan_w2(e):
                    def f_w2(eng, e=e):
                        eng.dma_start(out=w2s[:], in_=w2t[e]).then_inc(
                            sems["dW2"], 16)
                    wts = [("eT", w2_free[e - 1])] if e > 0 else []
                    stage("gpsimd", f_w2, waits=wts, dma=("dW2", 16))
                    w2_done[e] = cur("dW2")

                def plan_dispatch(e):
                    par = e % 2
                    rsn = "dR%d" % par

                    def f_rb(eng, e=e, par=par, rsn=rsn):
                        eng.dma_start(
                            out=comb_s[par][:],
                            in_=comb_h[e * NCAP:(e + 1) * NCAP, :]
                            .rearrange("(p c) v -> p c v", c=NB)
                        ).then_inc(sems[rsn], 16)

                    rwts = [("dG", p1_end["dG"])]
                    if e >= 2:
                        rwts.append(("eV", ye_done[e - 2]))
                        rwts.append(("eV", cast_done[e - 2]))
                    stage("sync", f_rb, waits=rwts, dma=(rsn, 16))
                    rb_done[e] = (rsn, cur(rsn))

                    def f_cast(eng, par=par):
                        eng.tensor_copy(idxi_s[par][:], comb_s[par][:, :, 0])
                        eng.tensor_copy(gat_s[par][:], comb_s[par][:, :, 1])
                    cwts = [rb_done[e]]
                    if e >= 2:
                        cwts.append(("dA", scat_done[(e - 1) * NB - 1]))
                    stage("vector", f_cast, waits=cwts, inc="eV")
                    cast_done[e] = cur("eV")

                    for c in range(NB):
                        psn = "dP%d" % (c % 2)

                        def f_gth(eng, e=e, c=c, par=par, psn=psn):
                            eng.indirect_dma_start(
                                gtm[c % 4][:], None, x2ntm[:],
                                IOX2(idxi_s[par][:, c:c + 1], 0),
                            ).then_inc(sems[psn], 16)
                        gwts = [("eV", cast_done[e])]
                        prevc = (e, c - 4) if c >= 4 else (e - 1, NB - 4 + c)
                        if prevc in tr_done:
                            gwts.append(("eT", tr_done[prevc]))
                        stage("gpsimd", f_gth, waits=gwts, dma=(psn, 16))
                        gath_done[(e, c)] = (psn, cur(psn))

                        def f_trn(eng, c=c):
                            for hc in range(HC):
                                bank = p6 if hc < 4 else p7
                                eng.matmul(
                                    bank[:, (hc % 4) * 128:(hc % 4 + 1) * 128],
                                    gtm[c % 4][:, hc * 128:(hc + 1) * 128],
                                    identb[:], start=True, stop=True)
                        twts = [gath_done[(e, c)]]
                        if e >= 1 and c == 0:
                            twts.append(("eA", sl_nt2.get(e - 1, 0)))
                            twts.append(("eV", g_nt2.get(e - 1, 0)))
                        if c > 0:
                            twts.append(("eV", tevv[(e, c - 1)]))
                        if e >= 2:
                            twts.append(("eT", h_done[e - 2]))
                        stage("tensor", f_trn, waits=twts, inc="eT")
                        tr_done[(e, c)] = cur("eT")

                        def f_tev(eng, e=e, c=c, par=par):
                            eng.tensor_copy(
                                xg[par][:, 0:4, c * 128:(c + 1) * 128],
                                p6[:, 0:512].rearrange("p (a b) -> p a b",
                                                       b=128))
                            eng.tensor_copy(
                                xg[par][:, 4:8, c * 128:(c + 1) * 128],
                                p7[:, 0:512].rearrange("p (a b) -> p a b",
                                                       b=128))
                        stage("vector", f_tev, waits=[("eT", tr_done[(e, c)])],
                              inc="eV")
                        tevv[(e, c)] = cur("eV")
                    tev_done[e] = cur("eV")

                silu_last = {}
                sl_g = {}
                g_last = {}
                sl_nt2 = {}
                g_nt2 = {}
                tev_done = {}
                tevv = {}
                for _ic4 in range(4):
                    plan_w13(0, _ic4, None)
                plan_w2(0)
                plan_dispatch(0)

                for e in range(E):
                    par = e % 2
                    for nt in range(NNT):
                        ssl = slice(nt * 512, (nt + 1) * 512)

                        for ic4 in range(4):
                            for m in range(4):
                                isl = slice(ic4 * 512 + m * 128,
                                            ic4 * 512 + (m + 1) * 128)

                                b3 = 4 + (m % 2 if nt == NNT - 1 else m)

                                def f_h(eng, isl=isl, m=m, par=par, ssl=ssl,
                                        b3=b3):
                                    for hc in range(HC):
                                        eng.matmul(pall[m][:],
                                                   w13[:, 0, hc, isl],
                                                   xg[par][:, hc, ssl],
                                                   start=(hc == 0),
                                                   stop=(hc == HC - 1))
                                    for hc in range(HC):
                                        eng.matmul(pall[b3][:],
                                                   w13[:, 1, hc, isl],
                                                   xg[par][:, hc, ssl],
                                                   start=(hc == 0),
                                                   stop=(hc == HC - 1))
                                wl = [w13_done[(e, ic4)]]
                                if ic4 == 0 and nt == 0:
                                    wl.append(("eV", tev_done[e]))
                                prevk = ((ic4 - 1) % 4, m)
                                if prevk in gv:
                                    wl.append(("eV", gv[prevk]))
                                if prevk in sl_g:
                                    wl.append(("eA", sl_g[prevk]))
                                if nt == NNT - 1 and m >= 2:
                                    wl.append(("eV", gv[(ic4, m - 2)]))
                                if (nt == NNT - 1 and m < 2 and ic4 > 0
                                        and (ic4 - 1, m + 2) in gv):
                                    wl.append(("eV", gv[(ic4 - 1, m + 2)]))
                                if ic4 == 0 and m < 2:
                                    yk = ((e, nt - 1, 6 + m) if nt > 0
                                          else (e - 1, NNT - 1, 6 + m))
                                    if yk in ye_done:
                                        wl.append(("eV", ye_done[yk]))
                                stage("tensor", f_h, waits=wl, inc="eT")
                                hdone = cur("eT")
                                if nt == NNT - 1 and m == 3 and e + 1 < E:
                                    plan_w13(e + 1, ic4, hdone)

                                stage("scalar",
                                      lambda eng, m=m:
                                      eng.activation(silu_s[:, m, :],
                                                     pall[m][:], AF.Silu),
                                      waits=[("eT", hdone)], inc="eA")
                                sl_g[(ic4, m)] = cur("eA")

                                def f_g(eng, ic4=ic4, m=m, b3=b3):
                                    eng.tensor_tensor(g_s[:, ic4 * 4 + m, :],
                                                      silu_s[:, m, :],
                                                      pall[b3][:], ALU.mult)
                                stage("vector", f_g, waits=W("eA"), inc="eV")
                                gv[(ic4, m)] = cur("eV")

                        if nt == NNT - 2:
                            sl_nt2[e] = cur("eA")
                            g_nt2[e] = cur("eV")
                        if nt == NNT - 1:
                            h_done[e] = cur("eT")
                            w13_free[e] = cur("eT")
                            silu_last[e] = cur("eA")
                            g_last[e] = cur("eV")
                            if e + 1 < E:
                                plan_dispatch(e + 1)

                        for cg in range(4):
                            c = nt * 4 + cg
                            for half in range(2):
                                bank = pall[(2 * cg + half) % 2]
                                hsl = slice(half * 512, (half + 1) * 512)

                                def f_y(eng, cg=cg, half=half, bank=bank,
                                        hsl=hsl):
                                    for icc in range(IC):
                                        eng.matmul(bank[:],
                                                   g_s[:, icc,
                                                       cg * 128:(cg + 1) * 128],
                                                   w2s[:, icc, hsl],
                                                   start=(icc == 0),
                                                   stop=(icc == IC - 1))
                                wl = [("dW2", w2_done[e])]
                                if cg == 0:
                                    if half == 0:
                                        wl.append(("eV", gv[(3, 3)]))
                                    wl.append(("eA", sl_g[(3, half)]))
                                key = (2 * cg + half - 2)
                                if key >= 0 and (e, nt, key) in ye_done:
                                    wl.append(("eV", ye_done[(e, nt, key)]))
                                stage("tensor", f_y, waits=wl, inc="eT")
                                ydone = cur("eT")

                                def f_ye(eng, c=c, half=half, bank=bank,
                                         par=par, hsl=hsl):
                                    eng.tensor_scalar(
                                        stg2[c % 2][:, hsl], bank[:],
                                        gat_s[par][:, c:c + 1], None,
                                        ALU.mult)
                                ywts = [("eT", ydone)]
                                if half == 0 and (e * NB + c - 2) in scat_done:
                                    ywts.append(("dA", scat_done[e * NB + c - 2]))
                                stage("vector", f_ye, waits=ywts, inc="eV")
                                ye_done[(e, nt, 2 * cg + half)] = cur("eV")

                            def f_sc(eng, c=c, par=par):
                                eng.indirect_dma_start(
                                    out[:], IOX2(idxi_s[par][:, c:c + 1], 0),
                                    stg2[c % 2][:], None,
                                    compute_op=ALU.add,
                                ).then_inc(sems["dA"], 16)
                            stage("gpsimd", f_sc,
                                  waits=[("eV", ye_done[(e, nt, 2 * cg + 1)]),
                                         ("dA", None),
                                         ("dG", p1_end["dG"])],
                                  dma=("dA", 16))
                            scat_done[e * NB + c] = cur("dA")

                    w2_free[e] = cur("eT")
                    ye_done[e] = ye_done[(e, NNT - 1, 7)]
                    if e + 1 < E:
                        plan_w2(e + 1)

        def f_fin(eng):
            for n in ("dG", "dX", "dA", "dW2", "dP0", "dP1"):
                if ct[n] > 0:
                    eng.wait_ge(sems[n], ct[n])
        blk.gpsimd(f_fin)

    return nc


def _consts():
    cst = np.zeros((128, 1544), np.float32)
    cst[:, 0:128] = np.eye(128, dtype=np.float32)
    cst[:, 128:256] = np.triu(np.ones((128, 128), np.float32))
    cst[:, 256:384] = 1.0
    bd = np.zeros((128, 2), np.float32)
    bd[0:64, 0] = 1.0
    bd[64:128, 1] = 1.0
    cst[:, 384:386] = bd
    cst[0, 386:514] = 1.0
    cst[:, 514] = 1e-5
    cst[:, 515] = 1e-24
    for e in range(8):
        cst[e, 520 + e * 128: 520 + (e + 1) * 128] = 1.0
    return cst


def kernel(L0_attn_x, L0_attn_Wq, L0_attn_Wk, L0_attn_Wv, L0_attn_Wa,
           L0_attn_ba, L0_attn_Wo, L0_attn_gamma, L0_attn_beta,
           L0_ffn_wg, L0_ffn_w1, L0_ffn_w3, L0_ffn_w2):
    x = np.asarray(L0_attn_x, np.float32)
    B = x.shape[0]

    def fmw(w):
        return np.ascontiguousarray(
            np.asarray(w, np.float32).reshape(8, 128, -1).transpose(1, 0, 2))

    wq_h = fmw(L0_attn_Wq)
    wk_h = fmw(L0_attn_Wk)
    wv_h = fmw(L0_attn_Wv)
    wa_h = fmw(L0_attn_Wa)
    wo_h = np.ascontiguousarray(
        np.asarray(L0_attn_Wo, np.float32).reshape(4, 128, 1024)
        .transpose(1, 0, 2))
    ba_h = np.ascontiguousarray(
        np.asarray(L0_attn_ba, np.float32).reshape(4, 128).T)
    gmb_h = np.ascontiguousarray(np.stack(
        [np.asarray(L0_attn_gamma, np.float32).reshape(8, 128).T,
         np.asarray(L0_attn_beta, np.float32).reshape(8, 128).T], axis=1))
    wgt_h = np.ascontiguousarray(
        np.asarray(L0_ffn_wg, np.float32).T.reshape(8, 128, 8)
        .transpose(1, 0, 2))
    w1t_h = np.ascontiguousarray(
        np.asarray(L0_ffn_w1, np.float32).transpose(0, 2, 1)
        .reshape(8, 8, 128, 2048).transpose(0, 2, 1, 3))
    w3t_h = np.ascontiguousarray(
        np.asarray(L0_ffn_w3, np.float32).transpose(0, 2, 1)
        .reshape(8, 8, 128, 2048).transpose(0, 2, 1, 3))
    w2t_h = np.ascontiguousarray(
        np.asarray(L0_ffn_w2, np.float32).transpose(0, 2, 1)
        .reshape(8, 16, 128, 1024).transpose(0, 2, 1, 3))
    cst_h = _consts()
    Tb = x.shape[1]
    tok_h = np.arange(Tb, dtype=np.float32).reshape(Tb // 128, 128).T.copy()

    nc = build_nc()
    in_maps = []
    for b in range(B):
        in_maps.append({
            "x_fm": np.ascontiguousarray(x[b].T),
            "wq": wq_h, "wk": wk_h, "wv": wv_h, "wa": wa_h, "ba": ba_h,
            "wo": wo_h, "gmb": gmb_h, "wgt": wgt_h,
            "w1t": w1t_h, "w3t": w3t_h, "w2t": w2t_h, "cst": cst_h,
            "tokc": tok_h,
        })
    res = run_bass_kernel_spmd(nc, in_maps, core_ids=list(range(B)))
    outs = np.stack([np.asarray(r["out"])[:Tb] for r in res.results], axis=0)
    return np.ascontiguousarray(outs.astype(np.float32))
